# version 16
# baseline (speedup 1.0000x reference)
import numpy as np
import concourse.bass as bass
import concourse.mybir as mybir
from concourse.bass_utils import run_bass_kernel_spmd
from concourse import dve_ops as _dve_ops
from concourse import dve_spec as _dve_spec
from concourse.dve_op import DveOp as _DveOp


def _mk_mulscan():
    name = "ANT_K_MULSCAN"
    for o in _dve_ops.OPS:
        if o.name == name:
            return o
    spec = _dve_spec.Spec(body=_dve_spec.scan(
        _dve_spec.AluOp.ADD, _dve_spec.Src0 * _dve_spec.Src1))
    shas = {}
    for ver in ("v3", "v4"):
        try:
            shas[ver] = _dve_spec.lower(spec, ver=ver, perf_en=False,
                                        name=name).sha(ver)
        except Exception:
            pass

    def _ref(in0, in1, c0, c1, c2):
        dt_ = np.float64 if in0.dtype == np.float64 else np.float32
        return np.cumsum(in0.astype(dt_) * in1.astype(dt_), axis=1)

    op = _DveOp(name, spec, subdim=False, uops_sha=shas, reference=_ref)
    _dve_ops.OPS.append(op)
    _dve_ops.CUSTOM_DVE_SPECS[name] = spec
    _dve_ops.CUSTOM_DVE_REFS[name] = _ref
    _dve_ops._SUB_OPCODE_FOR_NAME[name] = (
        _dve_ops._CUSTOM_DVE_ROW_BASE + len(_dve_ops.OPS) - 1)
    return op


MULSCAN = _mk_mulscan()

F32 = mybir.dt.float32
BF16 = mybir.dt.bfloat16
AF = mybir.ActivationFunctionType
ALU = mybir.AluOpType
AX = mybir.AxisListType

D, E, NS, RK, Lseq, H = 1024, 2048, 16, 64, 4096, 4096
DB, EB, HB = D // 128, E // 128, H // 128
EPS = 1e-5
Lc = 256
NCH = Lseq // Lc
MGRP = 4

ENGS = ("pe", "act", "dve", "pool", "sp")


class Prog:

    DSTREAMS = ("dma:wb0", "dma:wb1", "dma:wb2", "dma:xin", "dma:out",
                "dma:prep", "dma:small", "dma:tap")

    def __init__(self, nc):
        self.nc = nc
        self.scope = None
        self.steps = {e: [] for e in ENGS}
        self.cnt = {e: 0 for e in ENGS}
        self.dmacnt = {st: 0 for st in self.DSTREAMS}
        self.water = {e: {p: 0 for p in ENGS + self.DSTREAMS} for e in ENGS}
        self.sems = {}

    def alloc_sems(self, es):
        for e in ENGS:
            self.sems[e] = es.enter_context(self.nc.semaphore(f"S{e}"))
        for i, st in enumerate(self.DSTREAMS):
            self.sems[st] = es.enter_context(self.nc.semaphore(f"Sd{i}"))

    paranoid = False

    def need(self, eng, prod, val):
        if val <= 0 or self.water[eng][prod] >= val:
            return
        if prod == eng and not self.paranoid:
            return
        self.water[eng][prod] = val
        sem = self.sems[prod]
        self.steps[eng].append(lambda h, sem=sem, val=val: h.wait_ge(sem, val))

    def run(self, eng, fn):
        self.cnt[eng] += 1
        sem = self.sems[eng]

        def emit(h, fn=fn, sem=sem):
            fn(h).then_inc(sem, 1)
        emit._scope = self.scope
        self.steps[eng].append(emit)
        return (eng, self.cnt[eng])

    def dma(self, dst, src, stream="small", queue="sp"):
        st = "dma:" + stream
        sem = self.sems[st]
        prev = self.dmacnt[st]
        if prev:
            self.need(queue, st, prev)
        self.dmacnt[st] = prev + 16

        def emit(h, dst=dst, src=src, sem=sem):
            h.dma_start(dst, src).then_inc(sem, 16)
        emit._scope = self.scope
        self.steps[queue].append(emit)
        return (st, self.dmacnt[st])

    def emit_all(self, block):
        nc = self.nc
        for eng, attr in (("pe", "tensor"), ("act", "scalar"), ("dve", "vector"),
                          ("pool", "gpsimd"), ("sp", "sync")):
            steps = self.steps[eng]

            def body(h, steps=steps):
                cur = None
                cur_id = None
                for st in steps:
                    sc = getattr(st, "_scope", None)
                    if sc != cur:
                        if cur is not None:
                            nc.leave_named_scope(cur, cur_id, False)
                        if sc is not None:
                            cur_id, _ = nc.enter_named_scope(sc, False)
                        cur = sc
                    st(h)
                if cur is not None:
                    nc.leave_named_scope(cur, cur_id, False)
            getattr(block, attr)(body)


class Dep:

    def __init__(self):
        self.w = {}
        self.r = {}

    def rd(self, p, eng):
        for pe, v in self.w.items():
            p.need(eng, pe, v)

    def wr(self, p, eng):
        for pe, v in self.w.items():
            p.need(eng, pe, v)
        for pe, v in self.r.items():
            p.need(eng, pe, v)

    def did_rd(self, tok):
        eng, v = tok
        self.r[eng] = max(self.r.get(eng, 0), v)

    def did_wr(self, tok):
        eng, v = tok
        self.w[eng] = max(self.w.get(eng, 0), v)


def build(nch=NCH, taps=(), serialize=False, paranoid=False):
    nc = bass.Bass("TRN2", target_bir_lowering=False,
                   detect_race_conditions=paranoid)
    from contextlib import ExitStack
    es = ExitStack()
    M = Mk(nc, es, nch, set(taps))
    M.serialize = serialize
    M.p.paranoid = paranoid
    with es:
        M.declare()
        M.body()
        M.p.emit_all(M.blk)
    return nc, M.tapnames


class Mk:
    def __init__(self, nc, es, nch, taps):
        self.nc = nc
        self.es = es
        self.nch = nch
        self.taps = taps
        self.tapnames = []
        self.p = Prog(nc)
        self.dep = {}

    def D(self, name):
        return self.dep.setdefault(name, Dep())

    serialize = False

    def op(self, eng, fn, reads=(), writes=(), wchk=()):
        p = self.p
        if self.serialize:
            for o in ENGS:
                p.need(eng, o, p.cnt[o])
            for st in p.DSTREAMS:
                p.need(eng, st, p.dmacnt[st])
        for r in reads:
            self.D(r).rd(p, eng)
        for w in wchk:
            self.D(w).wr(p, eng)
        for w in writes:
            self.D(w).wr(p, eng)
        tok = p.run(eng, fn)
        for r in reads:
            self.D(r).did_rd(tok)
        for w in writes:
            self.D(w).did_wr(tok)
        return tok

    def dma(self, dst, src, reads=(), writes=(), stream="small", queue="sp"):
        p = self.p
        if self.serialize:
            for o in ENGS:
                p.need(queue, o, p.cnt[o])
            for st in p.DSTREAMS:
                p.need(queue, st, p.dmacnt[st])
        for r in reads:
            self.D(r).rd(p, queue)
        for w in writes:
            self.D(w).wr(p, queue)
        tok = p.dma(dst, src, stream, queue)
        for r in reads:
            self.D(r).did_rd(tok)
        for w in writes:
            self.D(w).did_wr(tok)
        return tok

    def tap(self, name, ap, shape, deps):
        if name not in self.taps:
            return
        t = self.nc.dram_tensor("tap_" + name, shape, ap.dtype,
                                kind="ExternalOutput").ap()
        self.tapnames.append("tap_" + name)
        self.dma(t, ap, reads=deps, writes=(), stream="tap")

    def declare(self):
        nc, es = self.nc, self.es
        self.hbm = {}

        def hin(name, shape):
            self.hbm[name] = nc.dram_tensor(name, shape, F32,
                                            kind="ExternalInput").ap()
        hin("x_t", [D, Lseq])
        self.out_h = nc.dram_tensor("out_t", [D, Lseq], F32,
                                    kind="ExternalOutput").ap()
        self.big = {}
        for nm, (K, Mm) in {"inA": (D, 2 * E), "inB": (D, 2 * E),
                            "poA": (E, D), "poB": (E, D),
                            "W1": (D, H), "W2": (H, D)}.items():
            hin(nm + "_lhsT", [K, Mm])
            kb, mb = K // 128, Mm // 128
            self.big[nm] = dict(K=K, M=Mm, KB=kb, MB=mb,
                                scr=nc.dram_tensor("scr_" + nm,
                                                   [128, kb * Mm], BF16).ap())
        self.scr_dtp = nc.dram_tensor("scr_dtp", [64, 2 * E], BF16).ap()
        for nm, sh in [("xpA_lhsT", [E, 96]), ("xpB_lhsT", [E, 96]),
                       ("dtpA_lhsT", [RK, E]), ("dtpB_lhsT", [RK, E]),
                       ("Alog_r", [128, 2 * EB * NS]), ("dtpb_r", [128, 2 * EB]),
                       ("cw_r", [128, 2 * EB * 4]), ("cb_r", [128, 2 * EB]),
                       ("Dsk_r", [128, 2 * EB]),
                       ("lnw_r", [128, 2 * DB]), ("lnb_r", [128, 2 * DB]),
                       ("nfw_r", [128, DB]), ("nfb_r", [128, DB]),
                       ("b1_r", [128, HB]), ("b2_r", [128, DB]),
                       ("sc4_r", [4, 1]), ("sc16_r", [16, 1]),
                       ("onesmean_r", [128, 1]), ("onesrow_r", [1, 128]),
                       ("epsc_r", [128, 1]), ("nhalf_r", [128, 1]),
                       ("delta16_r", [16, NS * 128]), ("ident_r", [128, 128]),
                       ("mask1_r", [4, HB]), ("mask2_r", [4, DB])]:
            hin(nm, sh)

        s = {}

        def sb(name, shape, dt):
            s[name] = es.enter_context(nc.sbuf_tensor(name, shape, dt))
        sb("x_res", [128, 3 * DB * Lc], F32)
        sb("t_str", [128, 3 * DB * Lc], F32)
        sb("xnf", [128, DB * Lc], F32)
        sb("xin_bf", [128, DB * Lc], BF16)
        sb("y_bf", [128, 2 * Lc], BF16)
        sb("xcin", [128, 2 * (Lc + 3)], F32)
        sb("tail3", [128, 2 * EB * 3], F32)
        sb("cconv", [128, 2 * Lc], F32)
        sb("sgtmp", [128, 2 * Lc], F32)
        sb("xc_bf", [128, 2 * EB * Lc], BF16)
        sb("zs_bf", [128, 2 * EB * Lc], BF16)
        sb("dtneg", [128, EB * Lc], BF16)
        sb("dtraw_bf", [64, Lc], BF16)
        sb("Bneg_bf", [16, Lc], BF16)
        sb("C_bf", [16, Lc], BF16)
        sb("B_bc", [128, NS * Lc], BF16)
        sb("C_bc", [128, NS * Lc], BF16)
        sb("u_bf", [128, 2 * Lc], BF16)
        sb("dA", [128, NS * Lc], F32)
        sb("dBu", [128, NS * Lc], BF16)
        sb("ytmp", [128, 2 * Lc], BF16)
        sb("carry", [128, 2 * EB * NS], F32)
        sb("hg_bf", [128, HB * Lc], BF16)
        sb("wbuf0", [128, DB * 128], BF16)
        sb("wbuf1", [128, DB * 128], BF16)
        sb("wbuf2", [128, 2048], BF16)
        sb("padt", [1, 16], F32)
        sb("xp_sb", [128, 2 * EB * 96], BF16)
        sb("dtp_s", [64, EB * 128], BF16)
        sb("Aneg", [128, 2 * EB * NS], F32)
        sb("ndtpb", [128, 2 * EB], F32)
        sb("cwsb", [128, 2 * EB * 4], F32)
        sb("cbc", [128, 2 * EB], F32)
        sb("Dsk", [128, 2 * EB], F32)
        sb("lnw", [128, 2 * DB], F32)
        sb("lnb", [128, 2 * DB], F32)
        sb("nfw", [128, DB], F32)
        sb("nfb", [128, DB], F32)
        sb("b1c", [128, HB], F32)
        sb("b2c", [128, DB], F32)
        sb("onesm", [128, 1], BF16)
        sb("epsc", [128, 1], F32)
        sb("nhalf", [128, 1], F32)
        sb("onesr_f", [1, 128], F32)
        sb("d16", [16, NS * 128], BF16)
        sb("sc4", [4, 1], F32)
        sb("sc16v", [16, 1], F32)
        sb("murow", [1, Lc], F32)
        sb("vrow", [1, Lc], F32)
        sb("rrow", [1, Lc], F32)
        sb("ident", [128, 128], BF16)
        sb("mask1", [4, HB], F32)
        sb("mask2", [4, DB], F32)
        sb("u_lt", [128, DB * 16], BF16)
        sb("T1sb", [4, D], BF16)
        sb("T2sb", [4, D], BF16)
        sb("coef_bf", [4, Lc], BF16)
        sb("zbt", [4, 8 * 128], BF16)
        sb("g4", [4, 4 * Lc], BF16)
        self.s = s

        raw = {}

        def psd(name, shape):
            raw[name] = es.enter_context(nc.psum_tensor(name, shape, F32))
        psd("ps_a", [128, Lc])
        psd("ps_b", [128, Lc])
        psd("ps_x", [16, 2 * Lc])
        psd("ps_y", [128, 2 * Lc])
        psd("ps_o", [128, DB * Lc])
        ps = {
            "ps_a": raw["ps_a"][:, :],
            "ps_b": raw["ps_b"][:, :],
            "ps_bm": raw["ps_x"][0:16, 0:Lc],
            "ps_cm": raw["ps_x"][0:16, Lc:2 * Lc],
            "ps_s1": raw["ps_x"][0:1, 0:Lc],
            "ps_s2": raw["ps_x"][0:1, Lc:2 * Lc],
            "ps_bc": raw["ps_y"][:, 0:Lc],
            "ps_bc2": raw["ps_y"][:, Lc:2 * Lc],
            "ps_dt": raw["ps_y"][0:64, Lc:2 * Lc],
            "ps_o": raw["ps_o"][:, :],
        }
        self.ps = ps
        for alias, owner in (("ps_bm", "ps_x"), ("ps_cm", "ps_x"),
                             ("ps_s1", "ps_x"), ("ps_s2", "ps_x"),
                             ("ps_bc", "ps_y"), ("ps_bc2", "ps_y"),
                             ("ps_dt", "ps_y")):
            self.dep[alias] = self.D(owner)

        self.p.alloc_sems(es)
        self.blk = es.enter_context(nc.Block())

    def bc8(self, psname):
        assert psname == "ps_bc"
        t = self.ps[psname].tensor
        return bass.AP(t, 0, [[2 * Lc, 128], [0, DB], [1, Lc]])

    def st0(self, ap_tensor, offset, reps, inner):
        return bass.AP(ap_tensor, offset, [[Lc, 128], [0, reps], [1, inner]])

    def body(self):
        self.p.scope = "prep"
        self.prepped = set()
        self.fifo = []
        self.wb2i = 0
        self.prescan_pushed = set()
        self.cur_pos = -1
        self.host_bdy = 0
        self.prep()
        nch = self.nch
        seq = [(0, 0)]
        for c in range(nch):
            if c + 1 < nch:
                seq.append((c + 1, 0))
            seq.append((c, 1))
        self.pos_of = {st: i for i, st in enumerate(seq)}
        self.xload(0)
        fa0, fb0 = self.ln0(0)
        fa0()
        fb0()
        self.ln0b(0)()
        self.push_prescan(0, 0, direct=True)
        if nch > 1:
            self.push_pack(1)
        self.bc_dt(0, 0)
        for i, (c, L_) in enumerate(seq):
            self.cur_pos = i
            self.host_bdy = 0
            self.p.scope = "bubble"
            self.drain_some(10 ** 9)
            nxt = seq[i + 1] if i + 1 < len(seq) else None
            if nxt is not None and nxt in self.prescan_pushed:
                self.push_bc_dt(nxt, host=(c, L_))
                nxt = None
            self.scanphase(c, L_)
            if L_ == 0:
                self.push_prescan(c, 1)
                if c == 1 and nch > 2:
                    self.push_pack(2)
            else:
                self.push_ffnblock(c)
                if c + 3 < nch:
                    self.push_pack(c + 3)
            if nxt is not None:
                self.push_bc_dt(nxt, host=(c, L_))
        self.cur_pos = len(seq)
        self.host_bdy = 0
        self.p.scope = "tail"
        self.drain_some(10 ** 9)
        self.p.scope = None
        p = self.p
        for st in p.DSTREAMS:
            v = p.dmacnt[st]
            if v:
                p.steps["sp"].append(
                    lambda h, sem=p.sems[st], v=v: h.wait_ge(sem, v))

    def push(self, fn, host=None, min_bdy=0):
        hp = self.pos_of[host] if host is not None else -1
        self.fifo.append([hp, min_bdy, fn])

    def drain_some(self, budget):
        n = 0
        while self.fifo and n < budget:
            hp, mb, fn = self.fifo[0]
            if hp > self.cur_pos or (hp == self.cur_pos and self.host_bdy < mb):
                break
            self.fifo.pop(0)
            fn()
            n += 1

    def push_pipe(self, entries, host=None):
        pend = None
        for d, cfn, mb in entries:
            if d is not None:
                self.push(d, host=host, min_bdy=max(0, mb - 1))
            if pend is not None:
                self.push(pend[0], host=host, min_bdy=pend[1])
            pend = (cfn, mb)
        if pend is not None:
            self.push(pend[0], host=host, min_bdy=pend[1])

    def tslot(self, c):
        sl = c % 3
        return (self.s["t_str"][:, sl * DB * Lc:(sl + 1) * DB * Lc],
                "t_str%d" % sl)

    def xslot(self, c):
        sl = c % 3
        return (self.s["x_res"][:, sl * DB * Lc:(sl + 1) * DB * Lc],
                "x_res%d" % sl)

    def prep(self):
        s, hbm, p = self.s, self.hbm, self.p
        op, dma = self.op, self.dma

        for src, dst in [("cw_r", "cwsb"), ("cb_r", "cbc"), ("Dsk_r", "Dsk"),
                         ("lnw_r", "lnw"), ("lnb_r", "lnb"), ("nfw_r", "nfw"),
                         ("nfb_r", "nfb"), ("b1_r", "b1c"), ("b2_r", "b2c"),
                         ("sc4_r", "sc4"), ("sc16_r", "sc16v"),
                         ("onesrow_r", "onesr_f"), ("mask1_r", "mask1"),
                         ("epsc_r", "epsc"), ("nhalf_r", "nhalf"),
                         ("mask2_r", "mask2")]:
            dma(s[dst][:, :], hbm[src], writes=(dst,))
        stg = s["dA"]
        dma(stg[:, 0:2 * EB * NS], hbm["Alog_r"], writes=("dA",))
        op("act", lambda h: h.activation(s["Aneg"][:, :], stg[:, 0:2 * EB * NS],
                                         AF.Exp),
           reads=("dA",), writes=("Aneg",))
        dma(stg[:, 0:2 * EB], hbm["dtpb_r"], writes=("dA",))
        op("dve", lambda h: h.tensor_scalar_mul(s["ndtpb"][:, :],
                                                stg[:, 0:2 * EB], -1.0),
           reads=("dA",), writes=("ndtpb",))
        dma(stg[:, 0:1], hbm["onesmean_r"], writes=("dA",))
        op("dve", lambda h: h.tensor_copy(s["onesm"][:, 0:1], stg[:, 0:1]),
           reads=("dA",), writes=("onesm",))
        dma(stg[0:16, 0:NS * 128], hbm["delta16_r"], writes=("dA",))
        op("dve", lambda h: h.tensor_copy(s["d16"][:, :], stg[0:16, 0:NS * 128]),
           reads=("dA",), writes=("d16",))
        dma(stg[:, 0:128], hbm["ident_r"], writes=("dA",))
        op("dve", lambda h: h.tensor_copy(s["ident"][:, :], stg[:, 0:128]),
           reads=("dA",), writes=("ident",))
        for L_, nm in ((0, "xpA_lhsT"), (1, "xpB_lhsT")):
            for k in range(EB):
                src2 = bass.AP(hbm[nm].tensor, k * 128 * 96, [[96, 128], [1, 96]])
                dma(stg[:, 0:96], src2, writes=("dA",))
                op("act", lambda h, L_=L_, k=k: h.activation(
                    s["xp_sb"][:, (L_ * EB + k) * 96:(L_ * EB + k + 1) * 96],
                    stg[:, 0:96], AF.Copy),
                   reads=("dA",), writes=("xp_sb",))
        for L_, nm in ((0, "dtpA_lhsT"), (1, "dtpB_lhsT")):
            dma(stg[0:64, 0:E], bass.AP(hbm[nm].tensor, 0, [[E, 64], [1, E]]),
                writes=("dA",))
            op("act", lambda h: h.activation(
                s["dBu"][:, :].bitcast(BF16)[0:64, 0:E],
                stg[0:64, 0:E], AF.Copy),
               reads=("dA",), writes=("dBu",))
            dma(bass.AP(self.scr_dtp.tensor, L_ * E, [[2 * E, 64], [1, E]]),
                s["dBu"][:, :].bitcast(BF16)[0:64, 0:E],
                reads=("dBu",), writes=("scr_dtp%d" % L_,))
        op("dve", lambda h: h.memset(s["tail3"][:, :], 0.0), writes=("tail3",))
        self.prep_slot = 0
        for nm in ("inA", "poA", "inB", "poB", "W1", "W2"):
            self.prep_big(nm)
        base = Dep()
        for nm2 in ("dA", "pstg0", "pstg1"):
            d0 = self.D(nm2)
            for e, v in d0.w.items():
                base.w[e] = max(base.w.get(e, 0), v)
            for e, v in d0.r.items():
                base.r[e] = max(base.r.get(e, 0), v)
        for i in range(2 * NS):
            d = Dep()
            d.w = dict(base.w)
            d.r = dict(base.r)
            self.dep["dAn%d" % i] = d
        tgt = self.D("dBu")
        for nm2 in ("pcv0", "pcv1"):
            d0 = self.D(nm2)
            for e, v in d0.w.items():
                tgt.w[e] = max(tgt.w.get(e, 0), v)
            for e, v in d0.r.items():
                tgt.r[e] = max(tgt.r.get(e, 0), v)

    def prep_big(self, nm):
        if nm in self.prepped:
            return
        self.prepped.add(nm)
        s, hbm, p = self.s, self.hbm, self.p
        op, dma = self.op, self.dma
        stg = s["dA"]
        oldscope = p.scope
        p.scope = "prep"
        owners_a = ["dA"] + ["dA%d" % n for n in range(NS)]
        owners_b = ["dBu"]
        for sl in range(4):
            for nm2, owners in (("pstg%d" % sl, owners_a), ("pcv%d" % sl, owners_b)):
                d = self.D(nm2)
                for ow in owners:
                    o = self.D(ow)
                    for e, v in o.w.items():
                        d.w[e] = max(d.w.get(e, 0), v)
                    for e, v in o.r.items():
                        d.r[e] = max(d.r.get(e, 0), v)
        if True:
            info = self.big[nm]
            lhsT, scr = hbm[nm + "_lhsT"], info["scr"]
            KB, Mm = info["KB"], info["M"]
            kmajor = nm in ("poA", "poB")
            units = []
            if kmajor:
                for k in range(KB):
                    units.append((bass.AP(lhsT.tensor, k * 128 * Mm,
                                          [[Mm, 128], [1, Mm]]),
                                  k * Mm, Mm))
            else:
                mg = 32 // KB
                gw = mg * 128
                ngrp = info["MB"] // mg
                per = min(2048 // gw, ngrp)
                for gg in range(ngrp // per):
                    for k in range(KB):
                        units.append((bass.AP(lhsT.tensor,
                                              k * 128 * Mm + gg * per * gw,
                                              [[Mm, 128], [1, per * gw]]),
                                      None, (gg, k, per, gw)))
            if kmajor:
                gsz = 2 * Mm
            else:
                gsz = KB * (32 // KB) * 128
            ldstreams = ("wb0", "wb1", "small")
            ststreams = ("prep", "xin", "out")
            for src2, dsto, meta in units:
                sl = self.prep_slot % 2
                six = self.prep_slot % 3
                self.prep_slot += 1
                if dsto is not None:
                    w = meta
                    ldsl = stg[:, sl * 2048:sl * 2048 + w]
                    cvsl = s["dBu"][:, sl * 2048:sl * 2048 + w]
                    dma(ldsl, src2, writes=("pstg%d" % sl,),
                        stream=ldstreams[six], queue="sp")
                    op("dve", lambda h, ldsl=ldsl, cvsl=cvsl: h.tensor_copy(
                        cvsl, ldsl),
                       reads=("pstg%d" % sl,), writes=("pcv%d" % sl,))
                    dst = bass.AP(scr.tensor, dsto, [[KB * Mm, 128], [1, w]])
                    gdep = "scr_%s_%d" % (nm, dsto // gsz)
                    dma(dst, cvsl, reads=("pcv%d" % sl,), writes=(gdep,),
                        stream=ststreams[six], queue="act")
                else:
                    gg, k, per, gw = meta
                    w = per * gw
                    ldsl = stg[:, sl * 2048:sl * 2048 + w]
                    cvsl = s["dBu"][:, sl * 2048:sl * 2048 + w]
                    dma(ldsl, src2, writes=("pstg%d" % sl,),
                        stream=ldstreams[six], queue="sp")
                    op("dve", lambda h, ldsl=ldsl, cvsl=cvsl: h.tensor_copy(
                        cvsl, ldsl),
                       reads=("pstg%d" % sl,), writes=("pcv%d" % sl,))
                    grp0 = gg * per
                    dst = bass.AP(scr.tensor,
                                  grp0 * (KB * gw) + k * gw,
                                  [[KB * Mm, 128], [KB * gw, per], [1, gw]])
                    gdeps = tuple("scr_%s_%d" % (nm, grp0 + j) for j in range(per))
                    dma(dst, cvsl, reads=("pcv%d" % sl,), writes=gdeps,
                        stream=ststreams[six], queue="act")
        for sl in range(4):
            for nm2 in ("pstg%d" % sl, "pcv%d" % sl):
                d = self.D(nm2)
                for tok_eng, tok_v in list(d.w.items()) + list(d.r.items()):
                    for ow in ["dA", "dBu"] + ["dA%d" % n for n in range(NS)]:
                        self.D(ow).did_rd((tok_eng, tok_v))

    def ln_stats(self, srcap, srcdep):
        self.ln_stats_a(srcap, srcdep)
        self.ln_stats_b()

    def ln_stats_a(self, srcap, srcdep):
        s, ps, op = self.s, self.ps, self.op
        op("act", lambda h: h.activation(s["xin_bf"][:, :], srcap, AF.Copy),
           reads=(srcdep,), writes=("xin_bf",))
        op("act", lambda h: h.activation(s["hg_bf"][:, 0:DB * Lc],
                                         s["xin_bf"][:, :], AF.Square),
           reads=("xin_bf",), writes=("hg_bf",))
        for k in range(DB):
            op("pe", lambda h, k=k: h.matmul(
                ps["ps_s1"][0:1, :], s["onesm"][:, 0:1],
                s["xin_bf"][:, k * Lc:(k + 1) * Lc],
                start=(k == 0), stop=(k == DB - 1)),
               reads=("onesm", "xin_bf"),
               writes=("ps_s1",) if k == DB - 1 else (),
               wchk=("ps_s1",) if k == 0 else ())
        for k in range(DB):
            op("pe", lambda h, k=k: h.matmul(
                ps["ps_s2"][0:1, :], s["onesm"][:, 0:1],
                s["hg_bf"][:, k * Lc:(k + 1) * Lc],
                start=(k == 0), stop=(k == DB - 1)),
               reads=("onesm", "hg_bf"),
               writes=("ps_s2",) if k == DB - 1 else (),
               wchk=("ps_s2",) if k == 0 else ())

    def ln_stats_b(self):
        s, ps, op = self.s, self.ps, self.op
        op("dve", lambda h: h.tensor_copy(s["murow"][0:1, :], ps["ps_s1"][0:1, :]),
           reads=("ps_s1",), writes=("murow",))
        op("dve", lambda h: h.tensor_mul(s["vrow"][0:1, :], s["murow"][0:1, :],
                                         s["murow"][0:1, :]),
           reads=("murow",), writes=("vrow",))
        op("dve", lambda h: h.tensor_sub(s["vrow"][0:1, :], ps["ps_s2"][0:1, :],
                                         s["vrow"][0:1, :]),
           reads=("ps_s2", "vrow"), writes=("vrow",))
        op("act", lambda h: h.activation(s["vrow"][0:1, :], s["vrow"][0:1, :],
                                         AF.Ln, bias=s["epsc"][0:1, 0:1]),
           reads=("vrow", "epsc"), writes=("vrow",))
        op("act", lambda h: h.activation(s["rrow"][0:1, :], s["vrow"][0:1, :],
                                         AF.Exp, scale=s["nhalf"][0:1, 0:1]),
           reads=("vrow", "nhalf"), writes=("rrow",))

    def ln_bcast(self, rowname, psname="ps_bc"):
        s, ps, op = self.s, self.ps, self.op
        op("pe", lambda h: h.matmul(ps[psname][:, :], s["onesr_f"][0:1, :],
                                    s[rowname][0:1, :], start=True, stop=True),
           reads=("onesr_f", rowname), writes=(psname,), wchk=(psname,))

    def xload(self, c):
        xap, xdep = self.xslot(c)
        src = bass.AP(self.hbm["x_t"].tensor, c * Lc,
                      [[Lseq, 128], [128 * Lseq, DB], [1, Lc]])
        self.dma(xap, src, writes=(xdep,), stream="xin")

    def ln0(self, c):
        def fa():
            self.p.scope = "ln0"
            xap, xdep = self.xslot(c)
            self.ln_stats_a(xap, xdep)

        def fb():
            self.p.scope = "ln0"
            self.ln_stats_b()
        return fa, fb

    def ln0b(self, c):
        def fn():
            self.p.scope = "ln0"
            s, op = self.s, self.op
            xap, xdep = self.xslot(c)
            tap_, tdep = self.tslot(c)
            self.ln_bcast("murow")
            op("dve", lambda h: h.tensor_sub(s["xnf"][:, :], xap,
                                             self.bc8("ps_bc")),
               reads=(xdep, "ps_bc"), writes=("xnf",))
            self.ln_bcast("rrow")
            op("dve", lambda h: h.tensor_mul(tap_, s["xnf"][:, :],
                                             self.bc8("ps_bc")),
               reads=("xnf", "ps_bc"), writes=(tdep,))
        return fn

    def push_pack(self, c):
        self.push(lambda c=c: self.xload(c))
        fa, fb = self.ln0(c)
        self.push(fa)
        self.push(fb)
        self.push(self.ln0b(c))
        self.push_prescan(c, 0)

    def lnin_u(self, c, L_):
        tap_, tdep = self.tslot(c)

        def f1a():
            self.p.scope = "lnin"
            self.ln_stats_a(tap_, tdep)

        def f1b():
            self.p.scope = "lnin"
            self.ln_stats_b()

        def f2():
            self.p.scope = "lnin"
            s, op = self.s, self.op
            lnoff = L_ * DB
            self.ln_bcast("murow")
            op("dve", lambda h: h.tensor_sub(s["xnf"][:, :], tap_,
                                             self.bc8("ps_bc")),
               reads=(tdep, "ps_bc"), writes=("xnf",))
            self.ln_bcast("rrow")
            op("dve", lambda h: h.tensor_mul(s["xnf"][:, :], s["xnf"][:, :],
                                             self.bc8("ps_bc")),
               reads=("xnf", "ps_bc"), writes=("xnf",))
            for k in range(DB):
                op("dve", lambda h, k=k: h.tensor_scalar(
                    s["xin_bf"][:, k * Lc:(k + 1) * Lc],
                    s["xnf"][:, k * Lc:(k + 1) * Lc],
                    s["lnw"][:, lnoff + k:lnoff + k + 1],
                    s["lnb"][:, lnoff + k:lnoff + k + 1],
                    ALU.mult, ALU.add),
                   reads=("xnf", "lnw", "lnb"), writes=("xin_bf",))
        return f1a, f1b, f2

    def push_prescan(self, c, L_, direct=False):
        host = (c - 1, L_) if (c - 1, L_) in self.pos_of else None
        f1a, f1b, f2 = self.lnin_u(c, L_)
        nm_in = "inA" if L_ == 0 else "inB"
        entries = [(None, f1a, 0), (None, f1b, 0), (None, f2, 0)]
        order = []
        for mblk in range(EB):
            for base in (0, EB):
                order.append((base + mblk, mblk + 1))
        prev = [None]

        def mk(ix):
            m, mb = order[ix]
            six = self.wb2i % 2
            self.wb2i += 1
            slot = six * 1024

            def dfn():
                self.p.scope = "inproj"
                s, dma = self.s, self.dma
                info = self.big[nm_in]
                KB = info["KB"]
                grp, j = divmod(m, 4)
                src = bass.AP(info["scr"].tensor, grp * (KB * 512) + j * 128,
                              [[KB * info["M"], 128], [512, KB], [1, 128]])
                dma(s["wbuf2"][:, slot:slot + KB * 128], src,
                    reads=("scr_%s_%d" % (nm_in, grp),),
                    writes=("wb2s%d" % six,), stream="wb2")

            def cfn():
                self.p.scope = "inproj"
                s, ps, op = self.s, self.ps, self.op
                info = self.big[nm_in]
                KB = info["KB"]
                bank = "ps_a" if m % 2 == 0 else "ps_b"
                if prev[0] is not None:
                    pm = prev[0]
                    pbank = "ps_a" if pm % 2 == 0 else "ps_b"
                    self.inproj_evac(c, L_, pm, pbank, L_ * EB)
                prev[0] = m
                for k in range(KB):
                    op("pe", lambda h, k=k, bank=bank: h.matmul(
                        ps[bank][:, :],
                        s["wbuf2"][:, slot + k * 128:slot + (k + 1) * 128],
                        s["xin_bf"][:, k * Lc:(k + 1) * Lc],
                        start=(k == 0), stop=(k == KB - 1)),
                       reads=("wb2s%d" % six, "xin_bf"),
                       writes=(bank,) if k == KB - 1 else (),
                       wchk=(bank,) if k == 0 else ())
            return dfn, cfn, mb

        for ix in range(len(order)):
            d, cf, mb = mk(ix)
            entries.append((d, cf, mb))

        def tail():
            self.p.scope = "inproj"
            pm = prev[0]
            pbank = "ps_a" if pm % 2 == 0 else "ps_b"
            self.inproj_evac(c, L_, pm, pbank, L_ * EB)
        entries.append((None, tail, EB))
        xa, xb = self.xproj_unit(c, L_)
        entries.append((None, xa, EB))
        entries.append((None, xb, EB))
        self.prescan_pushed.add((c, L_))
        if direct:
            for d, cfn, mb in entries:
                if d is not None:
                    d()
                cfn()
        else:
            self.push_pipe(entries, host=host)

    def xproj_unit(self, c, L_):
        def fn():
            self.p.scope = "xproj"
            s, ps, op = self.s, self.ps, self.op
            for k in range(EB):
                xoff = (L_ * EB + k) * 96
                for tgt, dep, c0, c1 in ((ps["ps_a"][0:64, :], "ps_a", 0, 64),
                                         (ps["ps_bm"][0:16, :], "ps_x", 64, 80),
                                         (ps["ps_b"][0:16, :], "ps_b", 80, 96)):
                    op("pe", lambda h, k=k, xoff=xoff, tgt=tgt, c0=c0, c1=c1:
                       h.matmul(tgt, s["xp_sb"][:, xoff + c0:xoff + c1],
                                s["xc_bf"][:, (L_ * EB + k) * Lc:
                                           (L_ * EB + k + 1) * Lc],
                                start=(k == 0), stop=(k == EB - 1)),
                       reads=("xp_sb", "xc%d" % L_),
                       writes=(dep,) if k == EB - 1 else (),
                       wchk=(dep,) if k == 0 else ())

        def fe():
            self.p.scope = "xproj"
            s, ps, op = self.s, self.ps, self.op
            op("act", lambda h: h.activation(s["dtraw_bf"][0:64, :],
                                             ps["ps_a"][0:64, :], AF.Copy),
               reads=("ps_a",), writes=("dtraw_bf",))
            op("dve", lambda h: h.tensor_scalar_mul(s["Bneg_bf"][0:16, :],
                                                    ps["ps_bm"][0:16, :], -1.0),
               reads=("ps_x",), writes=("Bneg_bf",))
            op("act", lambda h: h.activation(s["C_bf"][0:16, :],
                                             ps["ps_b"][0:16, :], AF.Copy),
               reads=("ps_b",), writes=("C_bf",))
        return fn, fe

    def xproj_unit_dead(self, c, L_):
        def fn():
            self.p.scope = "xproj"
            s, ps, op = self.s, self.ps, self.op
            for k in range(EB):
                xoff = (L_ * EB + k) * 96
                for tgt, dep, c0, c1 in ((ps["ps_a"][0:64, :], "ps_a", 0, 64),
                                         (ps["ps_bm"][0:16, :], "ps_x", 64, 80),
                                         (ps["ps_b"][0:16, :], "ps_b", 80, 96)):
                    op("pe", lambda h, k=k, xoff=xoff, tgt=tgt, c0=c0, c1=c1:
                       h.matmul(tgt, s["xp_sb"][:, xoff + c0:xoff + c1],
                                s["xc_bf"][:, (L_ * EB + k) * Lc:
                                           (L_ * EB + k + 1) * Lc],
                                start=(k == 0), stop=(k == EB - 1)),
                       reads=("xp_sb", "xc%d" % L_),
                       writes=(dep,) if k == EB - 1 else (),
                       wchk=(dep,) if k == 0 else ())
            op("act", lambda h: h.activation(s["dtraw_bf"][0:64, :],
                                             ps["ps_a"][0:64, :], AF.Copy),
               reads=("ps_a",), writes=("dtraw_bf",))
            op("dve", lambda h: h.tensor_scalar_mul(s["Bneg_bf"][0:16, :],
                                                    ps["ps_bm"][0:16, :], -1.0),
               reads=("ps_x",), writes=("Bneg_bf",))
            op("act", lambda h: h.activation(s["C_bf"][0:16, :],
                                             ps["ps_b"][0:16, :], AF.Copy),
               reads=("ps_b",), writes=("C_bf",))
        return fn

    def bc_dt(self, c, L_):
        for d, fn, mb in self.bc_dt_entries(c, L_):
            fn()

    def push_bc_dt(self, stage, host):
        c, L_ = stage
        for d, fn, mb in self.bc_dt_entries(c, L_):
            self.push(fn, host=host, min_bdy=mb)

    def bc_dt_entries(self, c, L_):
        s, ps, op = self.s, self.ps, self.op
        eoff = L_ * EB
        ents = []

        def dtp_load():
            self.p.scope = "dtproj"
            self.dma(s["dtp_s"][0:64, :],
                     bass.AP(self.scr_dtp.tensor, L_ * E, [[2 * E, 64], [1, E]]),
                     reads=("scr_dtp%d" % L_,), writes=("dtp_s",),
                     stream="small")
        ents.append((None, dtp_load, 10))

        def bcx():
            self.p.scope = "bcbuild"
            self._bcbuild(L_)
        ents.append((None, bcx, 16))

        def dtp():
            self.p.scope = "dtproj"
            for m in range(EB):
                bank = "ps_a" if m % 2 == 0 else "ps_b"
                op("pe", lambda h, m=m, bank=bank: h.matmul(
                    ps[bank][:, :], s["dtp_s"][0:64, m * 128:(m + 1) * 128],
                    s["dtraw_bf"][0:64, :], start=True, stop=True),
                   reads=("dtp_s", "dtraw_bf"), writes=(bank,), wchk=(bank,))
                op("act", lambda h, m=m, bank=bank: h.activation(
                    s["dtneg"][:, m * Lc:(m + 1) * Lc], ps[bank][:, :],
                    AF.Sigmoid,
                    scale=-1.0, bias=s["ndtpb"][:, eoff + m:eoff + m + 1]),
                   reads=(bank, "ndtpb"), writes=("dtneg",))
        ents.append((None, dtp, 16))

        def lnu():
            self.p.scope = "dtproj"
            op("act", lambda h: h.activation(s["dtneg"][:, :], s["dtneg"][:, :],
                                             AF.Ln),
               reads=("dtneg",), writes=("dtneg",))
            xco = L_ * EB * Lc
            op("dve", lambda h, xco=xco: h.tensor_mul(
                s["u_bf"][:, 0:Lc], s["dtneg"][:, 0:Lc],
                s["xc_bf"][:, xco:xco + Lc]),
               reads=("dtneg", "xc%d" % L_), writes=("u0",))
        ents.append((None, lnu, 16))
        return ents

    def _bcbuild(self, L_):
        s, ps, op = self.s, self.ps, self.op
        for n in range(NS):
            for psn, srcn, dstn in (("ps_bc", "Bneg_bf", "B_bc"),
                                    ("ps_bc2", "C_bf", "C_bc")):
                op("pe", lambda h, n=n, psn=psn, srcn=srcn: h.matmul(
                    ps[psn][:, :], s["d16"][:, n * 128:(n + 1) * 128],
                    s[srcn][0:16, :], start=True, stop=True),
                   reads=("d16", srcn), writes=(psn,), wchk=(psn,))
            for psn, srcn, dstn in (("ps_bc", "Bneg_bf", "B_bc"),
                                    ("ps_bc2", "C_bf", "C_bc")):
                op("act", lambda h, n=n, psn=psn, dstn=dstn: h.activation(
                    s[dstn][:, n * Lc:(n + 1) * Lc], ps[psn][:, :], AF.Copy),
                   reads=(psn,), writes=(dstn,))

    def stream_kgroups(self, nm, kg=2):
        info = self.big[nm]
        KB, MB = info["KB"], info["MB"]
        w = kg * MB * 128
        for grp in range(KB // kg):
            bufn = "wbuf%d" % (grp % 2)
            buf = self.s[bufn]
            src = bass.AP(info["scr"].tensor, grp * w,
                          [[KB * info["M"], 128], [1, w]])
            self.dma(buf[:, 0:w], src, reads=("scr_%s_%d" % (nm, grp),),
                     writes=(bufn,),
                     stream="wb0" if grp % 2 == 0 else "wb1")

            def tile(kig, m, buf=buf, MB=MB):
                off = kig * (MB * 128) + m * 128
                return buf[:, off:off + 128]
            yield grp, bufn, tile

    def scanphase(self, c, L_):
        s, ps, op = self.s, self.ps, self.op
        eoff = L_ * EB
        nm_po = "poA" if L_ == 0 else "poB"
        po_iter = self.stream_kgroups(nm_po, kg=1)
        for blk in range(EB):
            self.p.scope = "scan"
            self.scan_block(c, L_, blk, eoff)
            self.host_bdy = blk + 1
            self.drain_some(5)
            self.p.scope = "scan"
            grp, bufn, tile = next(po_iter)
            for m in range(DB):
                op("pe", lambda h, m=m, tile=tile, blk=blk: h.matmul(
                    ps["ps_o"][:, m * Lc:(m + 1) * Lc], tile(0, m),
                    s["y_bf"][:, (blk % 2) * Lc:(blk % 2 + 1) * Lc],
                    start=(blk == 0 and m % 2 == 0),
                    stop=(blk == EB - 1 and m % 2 == 1)),
                   reads=(bufn, "y%d" % (blk % 2)),
                   writes=("ps_o",) if (blk == EB - 1 and m == DB - 1) else (),
                   wchk=("ps_o",) if (blk == 0 and m == 0) else ())
        self.p.scope = "outproj"
        tap_, tdep = self.tslot(c)
        for m in range(DB):
            op("dve", lambda h, m=m: h.tensor_add(
                tap_[:, m * Lc:(m + 1) * Lc],
                tap_[:, m * Lc:(m + 1) * Lc],
                ps["ps_o"][:, m * Lc:(m + 1) * Lc]),
               reads=(tdep, "ps_o"), writes=(tdep,))

    def inproj_evac(self, c, L_, m, bank, eoff):
        s, ps, op = self.s, self.ps, self.op
        if m < EB:
            blk = m
            rot = blk % 2
            xoff = rot * (Lc + 3)
            op("dve", lambda h, blk=blk, xoff=xoff: h.tensor_copy(
                s["xcin"][:, xoff:xoff + 3],
                s["tail3"][:, (L_ * EB + blk) * 3:(L_ * EB + blk + 1) * 3]),
               reads=("tail3",), writes=("xcin%d" % rot,))
            op("dve", lambda h, xoff=xoff, bank=bank: h.tensor_copy(
                s["xcin"][:, xoff + 3:xoff + 3 + Lc], ps[bank][:, :]),
               reads=(bank,), writes=("xcin%d" % rot,))
            op("dve", lambda h, blk=blk, bank=bank: h.tensor_copy(
                s["tail3"][:, (L_ * EB + blk) * 3:(L_ * EB + blk + 1) * 3],
                ps[bank][:, Lc - 3:Lc]),
               reads=(bank,), writes=("tail3",))
            cvo = (L_ * EB + blk) * 4
            op("dve", lambda h, xoff=xoff, rot=rot, cvo=cvo, blk=blk:
               h.tensor_scalar(
                   s["cconv"][:, rot * Lc:(rot + 1) * Lc],
                   s["xcin"][:, xoff:xoff + Lc],
                   s["cwsb"][:, cvo:cvo + 1],
                   s["cbc"][:, eoff + blk:eoff + blk + 1],
                   ALU.mult, ALU.add),
               reads=("xcin%d" % rot, "cwsb", "cbc"), writes=("cconv%d" % rot,))
            for kk in range(1, 4):
                op("dve", lambda h, xoff=xoff, rot=rot, cvo=cvo, kk=kk:
                   h.scalar_tensor_tensor(
                       s["cconv"][:, rot * Lc:(rot + 1) * Lc],
                       s["xcin"][:, xoff + kk:xoff + kk + Lc],
                       s["cwsb"][:, cvo + kk:cvo + kk + 1],
                       s["cconv"][:, rot * Lc:(rot + 1) * Lc],
                       ALU.mult, ALU.add),
                   reads=("xcin%d" % rot, "cwsb", "cconv%d" % rot),
                   writes=("cconv%d" % rot,))
            op("act", lambda h, rot=rot: h.activation(
                s["sgtmp"][:, rot * Lc:(rot + 1) * Lc],
                s["cconv"][:, rot * Lc:(rot + 1) * Lc], AF.Sigmoid),
               reads=("cconv%d" % rot,), writes=("sgtmp%d" % rot,))
            op("dve", lambda h, rot=rot, blk=blk: h.tensor_mul(
                s["xc_bf"][:, (L_ * EB + blk) * Lc:(L_ * EB + blk + 1) * Lc],
                s["cconv"][:, rot * Lc:(rot + 1) * Lc],
                s["sgtmp"][:, rot * Lc:(rot + 1) * Lc]),
               reads=("cconv%d" % rot, "sgtmp%d" % rot),
               writes=("xc%d" % L_,))
        else:
            blk = m - EB
            rot = blk % 2
            op("act", lambda h, rot=rot, bank=bank: h.activation(
                s["sgtmp"][:, rot * Lc:(rot + 1) * Lc], ps[bank][:, :],
                AF.Sigmoid),
               reads=(bank,), writes=("sgtmp%d" % rot,))
            op("dve", lambda h, rot=rot, bank=bank, blk=blk: h.tensor_mul(
                s["zs_bf"][:, (L_ * EB + blk) * Lc:(L_ * EB + blk + 1) * Lc],
                ps[bank][:, :],
                s["sgtmp"][:, rot * Lc:(rot + 1) * Lc]),
               reads=(bank, "sgtmp%d" % rot), writes=("zs%d" % L_,))

    def scan_block(self, c, L_, blk, eoff):
        s, op = self.s, self.op
        rot = blk % 2
        aoff = (L_ * EB + blk) * NS
        dAfull = s["dA"][:, :].bitcast(BF16)
        dA = dAfull[:, rot * NS * Lc:(rot + 1) * NS * Lc]
        for n in range(NS):
            op("act", lambda h, n=n, blk=blk, aoff=aoff, dA=dA: h.activation(
                dA[:, n * Lc:(n + 1) * Lc],
                s["dtneg"][:, blk * Lc:(blk + 1) * Lc], AF.Exp,
                scale=s["Aneg"][:, aoff + n:aoff + n + 1]),
               reads=("dtneg", "Aneg"), writes=("dAn%d" % (rot * NS + n),))
        u0 = bass.AP(s["u_bf"][:, :].tensor, rot * Lc,
                     [[2 * Lc, 128], [0, NS], [1, Lc]])
        op("dve", lambda h, u0=u0: h.tensor_mul(s["dBu"][:, :], u0,
                                                s["B_bc"][:, :]),
           reads=("u%d" % rot, "B_bc"), writes=("dBu",))
        for n in range(NS):
            if c == 0:
                init = 0.0
            else:
                init = s["carry"][:, (aoff + n):(aoff + n) + 1]
            op("dve", lambda h, n=n, init=init, dA=dA: h.tensor_tensor_scan(
                s["dBu"][:, n * Lc:(n + 1) * Lc],
                dA[:, n * Lc:(n + 1) * Lc],
                s["dBu"][:, n * Lc:(n + 1) * Lc],
                init, ALU.mult, ALU.add),
               reads=("dAn%d" % (rot * NS + n), "dBu")
               + (("carry",) if c > 0 else ()),
               writes=("dBu",))
        if blk + 1 < EB:
            nrot = (blk + 1) % 2
            op("dve", lambda h, nrot=nrot, blk=blk: h.tensor_mul(
                s["u_bf"][:, nrot * Lc:(nrot + 1) * Lc],
                s["dtneg"][:, (blk + 1) * Lc:(blk + 2) * Lc],
                s["xc_bf"][:, (L_ * EB + blk + 1) * Lc:
                           (L_ * EB + blk + 2) * Lc]),
               reads=("dtneg", "xc%d" % L_), writes=("u%d" % nrot,))
        else:
            op("dve", lambda h, rot=rot: h.memset(
                s["ytmp"][:, (1 - rot) * Lc:(2 - rot) * Lc], 0.0),
               writes=("yt%d" % (1 - rot),))
        hlast = bass.AP(s["dBu"][:, :].tensor, Lc - 1, [[NS * Lc, 128], [Lc, NS]])
        op("dve", lambda h, hlast=hlast, aoff=aoff: h.tensor_copy(
            s["carry"][:, aoff:aoff + NS], hlast),
           reads=("dBu",), writes=("carry",))
        op("dve", lambda h: h.tensor_mul(s["dBu"][:, :], s["dBu"][:, :],
                                         s["C_bc"][:, :]),
           reads=("dBu", "C_bc"), writes=("dBu",))
        w_ = NS
        while w_ > 1:
            half = w_ // 2
            for j in range(half):
                op("dve", lambda h, j=j: h.tensor_add(
                    s["dBu"][:, j * Lc:(j + 1) * Lc],
                    s["dBu"][:, (2 * j) * Lc:(2 * j + 1) * Lc],
                    s["dBu"][:, (2 * j + 1) * Lc:(2 * j + 2) * Lc]),
                   reads=("dBu",), writes=("dBu",))
            w_ = half
        op("dve", lambda h, rot=rot, blk=blk: h.scalar_tensor_tensor(
            s["ytmp"][:, rot * Lc:(rot + 1) * Lc],
            s["xc_bf"][:, (L_ * EB + blk) * Lc:(L_ * EB + blk + 1) * Lc],
            s["Dsk"][:, eoff + blk:eoff + blk + 1],
            s["dBu"][:, 0:Lc], ALU.mult, ALU.add),
           reads=("xc%d" % L_, "Dsk", "dBu"), writes=("yt%d" % rot,))
        op("dve", lambda h, rot=rot, blk=blk: h.tensor_mul(
            s["y_bf"][:, rot * Lc:(rot + 1) * Lc],
            s["ytmp"][:, rot * Lc:(rot + 1) * Lc],
            s["zs_bf"][:, (L_ * EB + blk) * Lc:(L_ * EB + blk + 1) * Lc]),
           reads=("yt%d" % rot, "zs%d" % L_), writes=("y%d" % rot,))

    def lora_prep(self):
        s, ps, op = self.s, self.ps, self.op
        src = bass.AP(s["xin_bf"][:, :].tensor, 1,
                      [[DB * Lc, 128], [Lc, DB], [1, 16]])
        op("dve", lambda h, src=src: h.tensor_copy(s["u_lt"][:, :], src),
           reads=("xin_bf",), writes=("u_lt",))
        for tgt, base in (("T1sb", 0), ("T2sb", 8)):
            for blk in range(DB):
                inap = bass.AP(s["u_lt"][:, :].tensor, blk * 16 + base,
                               [[DB * 16, 128], [2, 4]])
                trout = ps["ps_bm"][0:4, 0:64].bitcast(BF16)
                op("pe", lambda h, inap=inap, trout=trout: h.transpose(
                    trout, inap, s["ident"][:, :]),
                   reads=("u_lt", "ident"), writes=("ps_bm",), wchk=("ps_bm",))
                op("act", lambda h, tgt=tgt, blk=blk, trout=trout: h.activation(
                    s[tgt][0:4, blk * 128:(blk + 1) * 128], trout, AF.Copy),
                   reads=("ps_bm",), writes=(tgt,))

    def push_ffnblock(self, c):
        s, ps, op = self.s, self.ps, self.op
        tap_, tdep = self.tslot(c)
        xap, xdep = self.xslot(c)
        host = (c + 2, 0) if (c + 2, 0) in self.pos_of else None

        def nf1a():
            self.p.scope = "normf"
            self.ln_stats_a(tap_, tdep)

        def nf1b():
            self.p.scope = "normf"
            self.ln_stats_b()

        def nf2():
            self.p.scope = "normf"
            self.ln_bcast("murow")
            op("dve", lambda h: h.tensor_sub(s["xnf"][:, :], tap_,
                                             self.bc8("ps_bc")),
               reads=(tdep, "ps_bc"), writes=("xnf",))
            self.ln_bcast("rrow")
            op("dve", lambda h: h.tensor_mul(s["xnf"][:, :], s["xnf"][:, :],
                                             self.bc8("ps_bc")),
               reads=("xnf", "ps_bc"), writes=("xnf",))
            for k in range(DB):
                op("dve", lambda h, k=k: h.tensor_scalar(
                    s["xnf"][:, k * Lc:(k + 1) * Lc],
                    s["xnf"][:, k * Lc:(k + 1) * Lc],
                    s["nfw"][:, k:k + 1], s["nfb"][:, k:k + 1],
                    ALU.mult, ALU.add),
                   reads=("xnf", "nfw", "nfb"), writes=("xnf",))
            op("dve", lambda h: h.tensor_add(xap, xap, s["xnf"][:, :]),
               reads=(xdep, "xnf"), writes=(xdep,))

        def lf1a():
            self.p.scope = "lnffn"
            self.ln_stats_a(xap, xdep)

        def lf1b():
            self.p.scope = "lnffn"
            self.ln_stats_b()

        def lf2():
            self.p.scope = "lnffn"
            self.ln_bcast("murow")
            op("dve", lambda h: h.tensor_sub(s["xnf"][:, :], xap,
                                             self.bc8("ps_bc")),
               reads=(xdep, "ps_bc"), writes=("xnf",))
            self.ln_bcast("rrow")
            op("dve", lambda h: h.tensor_mul(s["xin_bf"][:, :], s["xnf"][:, :],
                                             self.bc8("ps_bc")),
               reads=("xnf", "ps_bc"), writes=("xin_bf",))

        entries = [(None, nf1a), (None, nf1b), (None, nf2),
                   (None, lf1a), (None, lf1b), (None, lf2)]
        if c == 0:
            def lp():
                self.p.scope = "loraprep"
                self.lora_prep()
            entries.append((None, lp))

        def coefu():
            self.p.scope = "ffn"
            for k in range(DB):
                st = bass.AP(s["u_lt"][:, :].tensor, k * 16 + 1,
                             [[DB * 16, 128], [2, 4]])
                op("pe", lambda h, k=k, st=st: h.matmul(
                    ps["ps_bm"][0:4, :], st,
                    s["xin_bf"][:, k * Lc:(k + 1) * Lc],
                    start=(k == 0), stop=(k == DB - 1)),
                   reads=("u_lt", "xin_bf"),
                   writes=("ps_bm",) if k == DB - 1 else (),
                   wchk=("ps_bm",) if k == 0 else ())
            op("dve", lambda h: h.tensor_scalar_mul(s["coef_bf"][0:4, :],
                                                    ps["ps_bm"][0:4, :],
                                                    s["sc4"][0:4, 0:1]),
               reads=("ps_bm", "sc4"), writes=("coef_bf",))
        entries.append((None, coefu))

        w1info = self.big["W1"]
        prev1 = [None]

        def mk_w1(m):
            six = self.wb2i % 2
            self.wb2i += 1
            slot = six * 1024

            def dfn():
                self.p.scope = "ffn"
                grp, j = divmod(m, 4)
                KB = w1info["KB"]
                srcap = bass.AP(w1info["scr"].tensor,
                                grp * (KB * 512) + j * 128,
                                [[KB * w1info["M"], 128], [512, KB], [1, 128]])
                self.dma(s["wbuf2"][:, slot:slot + KB * 128], srcap,
                         reads=("scr_W1_%d" % grp,),
                         writes=("wb2s%d" % six,), stream="wb2")

            def gelu_of(pm):
                bank = "ps_a" if pm % 2 == 0 else "ps_b"
                op("act", lambda h, pm=pm, bank=bank: h.activation(
                    s["hg_bf"][:, pm * Lc:(pm + 1) * Lc], ps[bank][:, :],
                    AF.Gelu, bias=s["b1c"][:, pm:pm + 1]),
                   reads=(bank, "b1c"), writes=("hg_bf",))

            def cfn():
                self.p.scope = "ffn"
                KB = w1info["KB"]
                bank = "ps_a" if m % 2 == 0 else "ps_b"
                if prev1[0] is not None:
                    gelu_of(prev1[0])
                    prev1[0] = None
                for k in range(KB):
                    op("pe", lambda h, k=k, bank=bank: h.matmul(
                        ps[bank][:, :],
                        s["wbuf2"][:, slot + k * 128:slot + (k + 1) * 128],
                        s["xin_bf"][:, k * Lc:(k + 1) * Lc],
                        start=(k == 0), stop=False),
                       reads=("wb2s%d" % six, "xin_bf"),
                       wchk=(bank,) if k == 0 else ())
                zr = (m % 2) * 512
                op("dve", lambda h, m=m, zr=zr: h.tensor_scalar_mul(
                    s["zbt"][0:4, zr:zr + 128],
                    s["T1sb"][0:4, (m % 8) * 128:(m % 8 + 1) * 128],
                    s["mask1"][0:4, m:m + 1]),
                   reads=("T1sb", "mask1"), writes=("zbt%d" % (m % 2),))
                op("pe", lambda h, m=m, bank=bank, zr=zr: h.matmul(
                    ps[bank][:, :], s["zbt"][0:4, zr:zr + 128],
                    s["coef_bf"][0:4, :], start=False, stop=True),
                   reads=("zbt%d" % (m % 2), "coef_bf"), writes=(bank,))
                prev1[0] = m
            return dfn, cfn, gelu_of
        for m in range(2 * EB):
            d, cf, _ = mk_w1(m)
            entries.append((d, cf))

        def w1tail():
            self.p.scope = "ffn"
            pm = prev1[0]
            bank = "ps_a" if pm % 2 == 0 else "ps_b"
            op("act", lambda h, pm=pm, bank=bank: h.activation(
                s["hg_bf"][:, pm * Lc:(pm + 1) * Lc], ps[bank][:, :],
                AF.Gelu, bias=s["b1c"][:, pm:pm + 1]),
               reads=(bank, "b1c"), writes=("hg_bf",))
        entries.append((None, w1tail))
        for sp_ in range(4):
            entries.append((None, self.g_unit(sp_)))
        w2info = self.big["W2"]
        for m in range(DB):
            bank = "ps_a" if m % 2 == 0 else "ps_b"
            for q in range(4):
                six = self.wb2i % 2
                self.wb2i += 1
                slot = six * 1024

                def dfn(m=m, q=q, six=six, slot=slot):
                    self.p.scope = "ffn"
                    srcap = bass.AP(w2info["scr"].tensor, m * 4096 + q * 1024,
                                    [[32 * w2info["M"], 128], [1, 1024]])
                    self.dma(s["wbuf2"][:, slot:slot + 1024], srcap,
                             reads=("scr_W2_%d" % m,),
                             writes=("wb2s%d" % six,), stream="wb2")

                def cfn(m=m, q=q, six=six, slot=slot, bank=bank):
                    self.p.scope = "ffn"
                    for k in range(8):
                        kk = q * 8 + k
                        op("pe", lambda h, k=k, kk=kk, bank=bank, slot=slot:
                           h.matmul(
                               ps[bank][:, :],
                               s["wbuf2"][:, slot + k * 128:slot + (k + 1) * 128],
                               s["hg_bf"][:, kk * Lc:(kk + 1) * Lc],
                               start=(kk == 0), stop=False),
                           reads=("wb2s%d" % six, "hg_bf"),
                           wchk=(bank,) if kk == 0 else ())
                entries.append((dfn, cfn))

            def w2fin(m=m, bank=bank):
                self.p.scope = "ffn"
                zr = (m % 2) * 512
                for sp_ in range(4):
                    gat = bass.AP(s["T2sb"][:, :].tensor, (m % 2) * 512 + sp_,
                                  [[D, 4], [4, 128]])
                    op("dve", lambda h, m=m, sp_=sp_, gat=gat, zr=zr:
                       h.tensor_scalar_mul(
                           s["zbt"][0:4, zr + sp_ * 128:zr + (sp_ + 1) * 128],
                           gat, s["mask2"][0:4, m:m + 1]),
                       reads=("T2sb", "mask2"), writes=("zbt%d" % (m % 2),))
                for sp_ in range(4):
                    op("pe", lambda h, m=m, sp_=sp_, bank=bank, zr=zr: h.matmul(
                        ps[bank][:, :],
                        s["zbt"][0:4, zr + sp_ * 128:zr + (sp_ + 1) * 128],
                        s["g4"][0:4, sp_ * Lc:(sp_ + 1) * Lc],
                        start=False, stop=(sp_ == 3)),
                       reads=("zbt%d" % (m % 2), "g4"),
                       writes=(bank,) if sp_ == 3 else ())
                op("dve", lambda h, m=m, bank=bank: h.scalar_tensor_tensor(
                    xap[:, m * Lc:(m + 1) * Lc], ps[bank][:, :],
                    s["b2c"][:, m:m + 1],
                    xap[:, m * Lc:(m + 1) * Lc], ALU.add, ALU.add),
                   reads=(bank, "b2c", xdep), writes=(xdep,))
            entries.append((None, w2fin))

        def od():
            dst = bass.AP(self.out_h.tensor, c * Lc,
                          [[Lseq, 128], [128 * Lseq, DB], [1, Lc]])
            self.dma(dst, xap, reads=(xdep,), stream="out")
        entries.append((None, od))
        piped = [(d, f, 1 + i // 4) for i, (d, f) in enumerate(entries)]
        self.push_pipe(piped, host=host)

    def w1_unit(self, u):
        def fn():
            self.p.scope = "ffn"
            s, ps, op, dma = self.s, self.ps, self.op, self.dma
            info = self.big["W1"]
            KB = info["KB"]
            m0 = 2 * u
            grp = m0 // 4
            half = (m0 % 4) // 2
            src = bass.AP(info["scr"].tensor, grp * (KB * 512) + half * 256,
                          [[KB * info["M"], 128], [512, KB], [1, 256]])
            dma(s["wbuf2"][:, 0:KB * 256], src,
                reads=("scr_W1_%d" % grp,), writes=("wbuf2",), stream="wb2")
            for mi in range(2):
                m = m0 + mi
                bank = "ps_a" if m % 2 == 0 else "ps_b"
                for k in range(KB):
                    op("pe", lambda h, k=k, mi=mi, bank=bank: h.matmul(
                        ps[bank][:, :],
                        s["wbuf2"][:, k * 256 + mi * 128:k * 256 + mi * 128 + 128],
                        s["xin_bf"][:, k * Lc:(k + 1) * Lc],
                        start=(k == 0), stop=False),
                       reads=("wbuf2", "xin_bf"),
                       wchk=(bank,) if k == 0 else ())
                zr = (m % 2) * 512
                op("dve", lambda h, m=m, zr=zr: h.tensor_scalar_mul(
                    s["zbt"][0:4, zr:zr + 128],
                    s["T1sb"][0:4, (m % 8) * 128:(m % 8 + 1) * 128],
                    s["mask1"][0:4, m:m + 1]),
                   reads=("T1sb", "mask1"), writes=("zbt%d" % (m % 2),))
                op("pe", lambda h, m=m, bank=bank, zr=zr: h.matmul(
                    ps[bank][:, :], s["zbt"][0:4, zr:zr + 128],
                    s["coef_bf"][0:4, :], start=False, stop=True),
                   reads=("zbt%d" % (m % 2), "coef_bf"), writes=(bank,))
                op("act", lambda h, m=m, bank=bank: h.activation(
                    s["hg_bf"][:, m * Lc:(m + 1) * Lc], ps[bank][:, :], AF.Gelu,
                    bias=s["b1c"][:, m:m + 1]),
                   reads=(bank, "b1c"), writes=("hg_bf",))
        return fn

    def g_unit(self, sp_):
        def fn():
            self.p.scope = "ffn"
            s, op = self.s, self.op
            psn = ("ps_bm", "ps_cm", "ps_bc", "ps_dt")[sp_]
            for i, kb in enumerate(range(sp_ * 8, sp_ * 8 + 8)):
                st = bass.AP(s["u_lt"][:, :].tensor, (kb % 8) * 16 + 9,
                             [[DB * 16, 128], [2, 4]])
                op("pe", lambda h, kb=kb, st=st, psn=psn, i=i: h.matmul(
                    self.ps[psn][0:4, :],
                    st, s["hg_bf"][:, kb * Lc:(kb + 1) * Lc],
                    start=(i == 0), stop=(i == 7)),
                   reads=("u_lt", "hg_bf"), writes=(psn,) if i == 7 else (),
                   wchk=(psn,) if i == 0 else ())
            op("dve", lambda h, sp_=sp_, psn=psn: h.tensor_scalar_mul(
                s["g4"][0:4, sp_ * Lc:(sp_ + 1) * Lc],
                self.ps[psn][0:4, :],
                s["sc4"][0:4, 0:1]),
               reads=(psn, "sc4"), writes=("g4",))
        return fn

    def w2_unit(self, c, m):
        def fn():
            self.p.scope = "ffn"
            s, ps, op, dma = self.s, self.ps, self.op, self.dma
            info = self.big["W2"]
            xap, xdep = self.xslot(c)
            bank = "ps_a" if m % 2 == 0 else "ps_b"
            for hf in range(2):
                src = bass.AP(info["scr"].tensor, m * 4096 + hf * 2048,
                              [[32 * info["M"], 128], [1, 2048]])
                dma(s["wbuf2"][:, 0:2048], src,
                    reads=("scr_W2_%d" % m,), writes=("wbuf2",), stream="wb2")
                for k in range(16):
                    kk = hf * 16 + k
                    op("pe", lambda h, k=k, kk=kk, bank=bank: h.matmul(
                        ps[bank][:, :], s["wbuf2"][:, k * 128:(k + 1) * 128],
                        s["hg_bf"][:, kk * Lc:(kk + 1) * Lc],
                        start=(kk == 0), stop=False),
                       reads=("wbuf2", "hg_bf"),
                       wchk=(bank,) if kk == 0 else ())
            zr = (m % 2) * 512
            for sp_ in range(4):
                gat = bass.AP(s["T2sb"][:, :].tensor, (m % 2) * 512 + sp_,
                              [[D, 4], [4, 128]])
                op("dve", lambda h, m=m, sp_=sp_, gat=gat, zr=zr:
                   h.tensor_scalar_mul(
                       s["zbt"][0:4, zr + sp_ * 128:zr + (sp_ + 1) * 128],
                       gat, s["mask2"][0:4, m:m + 1]),
                   reads=("T2sb", "mask2"), writes=("zbt%d" % (m % 2),))
            for sp_ in range(4):
                op("pe", lambda h, m=m, sp_=sp_, bank=bank, zr=zr: h.matmul(
                    ps[bank][:, :],
                    s["zbt"][0:4, zr + sp_ * 128:zr + (sp_ + 1) * 128],
                    s["g4"][0:4, sp_ * Lc:(sp_ + 1) * Lc],
                    start=False, stop=(sp_ == 3)),
                   reads=("zbt%d" % (m % 2), "g4"),
                   writes=(bank,) if sp_ == 3 else ())
            op("dve", lambda h, m=m, bank=bank: h.scalar_tensor_tensor(
                xap[:, m * Lc:(m + 1) * Lc], ps[bank][:, :],
                s["b2c"][:, m:m + 1],
                xap[:, m * Lc:(m + 1) * Lc], ALU.add, ALU.add),
               reads=(bank, "b2c", xdep), writes=(xdep,))
        return fn


def make_inputs(inp):
    f = np.float32
    A = np.ascontiguousarray
    outs = []
    const = {
        "sc4_r": np.repeat(inp["L0_ffn_scale"].reshape(1, 1), 4, 0).astype(f),
        "sc16_r": np.repeat(inp["L0_ffn_scale"].reshape(1, 1), 16, 0).astype(f),
        "onesmean_r": np.full((128, 1), 1.0 / D, f),
        "epsc_r": np.full((128, 1), EPS, f),
        "nhalf_r": np.full((128, 1), -0.5, f),
        "onesrow_r": np.ones((1, 128), f),
        "ident_r": np.eye(128, dtype=f),
    }
    d16 = np.zeros((16, NS * 128), f)
    for n in range(NS):
        d16[n, n * 128:(n + 1) * 128] = 1.0
    const["delta16_r"] = d16
    m1 = np.zeros((4, HB), f)
    for b2 in range(HB):
        m1[b2 // 8, b2] = 1.0
    const["mask1_r"] = m1
    m2 = np.zeros((4, DB), f)
    for b in range(DB):
        m2[b // 2, b] = 1.0
    const["mask2_r"] = m2

    def r3(x, nb):
        return A(x.reshape(2, nb, 128).transpose(2, 0, 1).reshape(128, 2 * nb))

    def r1(x, nb):
        return A(x.reshape(nb, 128).T)

    w = {
        "inA_lhsT": A(inp["L0_attn_in_proj_w"][0].T),
        "inB_lhsT": A(inp["L0_attn_in_proj_w"][1].T),
        "poA_lhsT": A(inp["L0_attn_out_proj_w"][0].T),
        "poB_lhsT": A(inp["L0_attn_out_proj_w"][1].T),
        "W1_lhsT": A(inp["L0_ffn_W1"].T),
        "W2_lhsT": A(inp["L0_ffn_W2"].T),
        "xpA_lhsT": A(inp["L0_attn_x_proj_w"][0].T),
        "xpB_lhsT": A(inp["L0_attn_x_proj_w"][1].T),
        "dtpA_lhsT": A(inp["L0_attn_dt_proj_w"][0].T),
        "dtpB_lhsT": A(inp["L0_attn_dt_proj_w"][1].T),
        "Alog_r": A(inp["L0_attn_A_log"].reshape(2, EB, 128, NS)
                    .transpose(2, 0, 1, 3).reshape(128, 2 * EB * NS)),
        "dtpb_r": r3(inp["L0_attn_dt_proj_b"], EB),
        "cw_r": A(inp["L0_attn_conv_w"].reshape(2, EB, 128, 4)
                  .transpose(2, 0, 1, 3).reshape(128, 2 * EB * 4)),
        "cb_r": r3(inp["L0_attn_conv_b"], EB),
        "Dsk_r": r3(inp["L0_attn_D_skip"], EB),
        "lnw_r": r3(inp["L0_attn_ln_w"], DB),
        "lnb_r": r3(inp["L0_attn_ln_b"], DB),
        "nfw_r": r1(inp["L0_attn_norm_f_w"], DB),
        "nfb_r": r1(inp["L0_attn_norm_f_b"], DB),
        "b1_r": r1(inp["L0_ffn_b1"], HB),
        "b2_r": r1(inp["L0_ffn_b2"], DB),
    }
    w.update(const)
    x = inp["L0_attn_x"]
    in_maps = []
    for b in range(x.shape[0]):
        m = dict(w)
        m["x_t"] = A(x[b].T)
        in_maps.append(m)
    return in_maps


def kernel(**inputs):
    in_maps = make_inputs(inputs)
    nc, _ = build()
    res = run_bass_kernel_spmd(nc, in_maps, core_ids=list(range(len(in_maps))))
    return np.stack([np.ascontiguousarray(r["out_t"].T) for r in res.results])


# version 17 (final)
# speedup vs baseline: 1.0233x; 1.0233x over previous
import numpy as np
import concourse.bass as bass
import concourse.mybir as mybir
from concourse.bass_utils import run_bass_kernel_spmd
from concourse import dve_ops as _dve_ops
from concourse import dve_spec as _dve_spec
from concourse.dve_op import DveOp as _DveOp


def _mk_mulscan():
    name = "ANT_K_MULSCAN"
    for o in _dve_ops.OPS:
        if o.name == name:
            return o
    spec = _dve_spec.Spec(body=_dve_spec.scan(
        _dve_spec.AluOp.ADD, _dve_spec.Src0 * _dve_spec.Src1))
    shas = {}
    for ver in ("v3", "v4"):
        try:
            shas[ver] = _dve_spec.lower(spec, ver=ver, perf_en=False,
                                        name=name).sha(ver)
        except Exception:
            pass

    def _ref(in0, in1, c0, c1, c2):
        dt_ = np.float64 if in0.dtype == np.float64 else np.float32
        return np.cumsum(in0.astype(dt_) * in1.astype(dt_), axis=1)

    op = _DveOp(name, spec, subdim=False, uops_sha=shas, reference=_ref)
    _dve_ops.OPS.append(op)
    _dve_ops.CUSTOM_DVE_SPECS[name] = spec
    _dve_ops.CUSTOM_DVE_REFS[name] = _ref
    _dve_ops._SUB_OPCODE_FOR_NAME[name] = (
        _dve_ops._CUSTOM_DVE_ROW_BASE + len(_dve_ops.OPS) - 1)
    return op


MULSCAN = _mk_mulscan()

F32 = mybir.dt.float32
BF16 = mybir.dt.bfloat16
AF = mybir.ActivationFunctionType
ALU = mybir.AluOpType
AX = mybir.AxisListType

D, E, NS, RK, Lseq, H = 1024, 2048, 16, 64, 4096, 4096
DB, EB, HB = D // 128, E // 128, H // 128
EPS = 1e-5
Lc = 256
NCH = Lseq // Lc
MGRP = 4

ENGS = ("pe", "act", "dve", "pool", "sp")


class Prog:

    DSTREAMS = ("dma:wb0", "dma:wb1", "dma:wb2", "dma:xin", "dma:out",
                "dma:prep", "dma:small", "dma:tap")

    def __init__(self, nc):
        self.nc = nc
        self.scope = None
        self.steps = {e: [] for e in ENGS}
        self.cnt = {e: 0 for e in ENGS}
        self.dmacnt = {st: 0 for st in self.DSTREAMS}
        self.water = {e: {p: 0 for p in ENGS + self.DSTREAMS} for e in ENGS}
        self.sems = {}

    def alloc_sems(self, es):
        for e in ENGS:
            self.sems[e] = es.enter_context(self.nc.semaphore(f"S{e}"))
        for i, st in enumerate(self.DSTREAMS):
            self.sems[st] = es.enter_context(self.nc.semaphore(f"Sd{i}"))

    paranoid = False

    def need(self, eng, prod, val):
        if val <= 0 or self.water[eng][prod] >= val:
            return
        if prod == eng and not self.paranoid:
            return
        self.water[eng][prod] = val
        sem = self.sems[prod]
        self.steps[eng].append(lambda h, sem=sem, val=val: h.wait_ge(sem, val))

    def run(self, eng, fn):
        self.cnt[eng] += 1
        sem = self.sems[eng]

        def emit(h, fn=fn, sem=sem):
            fn(h).then_inc(sem, 1)
        emit._scope = self.scope
        self.steps[eng].append(emit)
        return (eng, self.cnt[eng])

    def dma(self, dst, src, stream="small", queue="sp"):
        st = "dma:" + stream
        sem = self.sems[st]
        prev = self.dmacnt[st]
        if prev:
            self.need(queue, st, prev)
        self.dmacnt[st] = prev + 16

        def emit(h, dst=dst, src=src, sem=sem):
            h.dma_start(dst, src).then_inc(sem, 16)
        emit._scope = self.scope
        self.steps[queue].append(emit)
        return (st, self.dmacnt[st])

    def emit_all(self, block):
        nc = self.nc
        for eng, attr in (("pe", "tensor"), ("act", "scalar"), ("dve", "vector"),
                          ("pool", "gpsimd"), ("sp", "sync")):
            steps = self.steps[eng]

            def body(h, steps=steps):
                cur = None
                cur_id = None
                for st in steps:
                    sc = getattr(st, "_scope", None)
                    if sc != cur:
                        if cur is not None:
                            nc.leave_named_scope(cur, cur_id, False)
                        if sc is not None:
                            cur_id, _ = nc.enter_named_scope(sc, False)
                        cur = sc
                    st(h)
                if cur is not None:
                    nc.leave_named_scope(cur, cur_id, False)
            getattr(block, attr)(body)


class Dep:

    def __init__(self):
        self.w = {}
        self.r = {}

    def rd(self, p, eng):
        for pe, v in self.w.items():
            p.need(eng, pe, v)

    def wr(self, p, eng):
        for pe, v in self.w.items():
            p.need(eng, pe, v)
        for pe, v in self.r.items():
            p.need(eng, pe, v)

    def did_rd(self, tok):
        eng, v = tok
        self.r[eng] = max(self.r.get(eng, 0), v)

    def did_wr(self, tok):
        eng, v = tok
        self.w[eng] = max(self.w.get(eng, 0), v)


def build(nch=NCH, taps=(), serialize=False, paranoid=False):
    nc = bass.Bass("TRN2", target_bir_lowering=False,
                   detect_race_conditions=paranoid)
    from contextlib import ExitStack
    es = ExitStack()
    M = Mk(nc, es, nch, set(taps))
    M.serialize = serialize
    M.p.paranoid = paranoid
    with es:
        M.declare()
        M.body()
        M.p.emit_all(M.blk)
    return nc, M.tapnames


class Mk:
    def __init__(self, nc, es, nch, taps):
        self.nc = nc
        self.es = es
        self.nch = nch
        self.taps = taps
        self.tapnames = []
        self.p = Prog(nc)
        self.dep = {}

    def D(self, name):
        return self.dep.setdefault(name, Dep())

    serialize = False

    def op(self, eng, fn, reads=(), writes=(), wchk=()):
        p = self.p
        if self.serialize:
            for o in ENGS:
                p.need(eng, o, p.cnt[o])
            for st in p.DSTREAMS:
                p.need(eng, st, p.dmacnt[st])
        for r in reads:
            self.D(r).rd(p, eng)
        for w in wchk:
            self.D(w).wr(p, eng)
        for w in writes:
            self.D(w).wr(p, eng)
        tok = p.run(eng, fn)
        for r in reads:
            self.D(r).did_rd(tok)
        for w in writes:
            self.D(w).did_wr(tok)
        return tok

    def dma(self, dst, src, reads=(), writes=(), stream="small", queue="sp"):
        p = self.p
        if self.serialize:
            for o in ENGS:
                p.need(queue, o, p.cnt[o])
            for st in p.DSTREAMS:
                p.need(queue, st, p.dmacnt[st])
        for r in reads:
            self.D(r).rd(p, queue)
        for w in writes:
            self.D(w).wr(p, queue)
        tok = p.dma(dst, src, stream, queue)
        for r in reads:
            self.D(r).did_rd(tok)
        for w in writes:
            self.D(w).did_wr(tok)
        return tok

    def tap(self, name, ap, shape, deps):
        if name not in self.taps:
            return
        t = self.nc.dram_tensor("tap_" + name, shape, ap.dtype,
                                kind="ExternalOutput").ap()
        self.tapnames.append("tap_" + name)
        self.dma(t, ap, reads=deps, writes=(), stream="tap")

    def declare(self):
        nc, es = self.nc, self.es
        self.hbm = {}

        def hin(name, shape):
            self.hbm[name] = nc.dram_tensor(name, shape, F32,
                                            kind="ExternalInput").ap()
        hin("x_t", [D, Lseq])
        self.out_h = nc.dram_tensor("out_t", [D, Lseq], F32,
                                    kind="ExternalOutput").ap()
        self.big = {}
        for nm, (K, Mm) in {"inA": (D, 2 * E), "inB": (D, 2 * E),
                            "poA": (E, D), "poB": (E, D),
                            "W1": (D, H), "W2": (H, D)}.items():
            hin(nm + "_lhsT", [K, Mm])
            kb, mb = K // 128, Mm // 128
            self.big[nm] = dict(K=K, M=Mm, KB=kb, MB=mb,
                                scr=nc.dram_tensor("scr_" + nm,
                                                   [128, kb * Mm], BF16).ap())
        self.scr_dtp = nc.dram_tensor("scr_dtp", [64, 2 * E], BF16).ap()
        for nm, sh in [("xpA_lhsT", [E, 96]), ("xpB_lhsT", [E, 96]),
                       ("dtpA_lhsT", [RK, E]), ("dtpB_lhsT", [RK, E]),
                       ("Alog_r", [128, 2 * EB * NS]), ("dtpb_r", [128, 2 * EB]),
                       ("cw_r", [128, 2 * EB * 4]), ("cb_r", [128, 2 * EB]),
                       ("Dsk_r", [128, 2 * EB]),
                       ("lnw_r", [128, 2 * DB]), ("lnb_r", [128, 2 * DB]),
                       ("nfw_r", [128, DB]), ("nfb_r", [128, DB]),
                       ("b1_r", [128, HB]), ("b2_r", [128, DB]),
                       ("sc4_r", [4, 1]), ("sc16_r", [16, 1]),
                       ("onesmean_r", [128, 1]), ("onesrow_r", [1, 128]),
                       ("epsc_r", [128, 1]), ("nhalf_r", [128, 1]),
                       ("delta16_r", [16, NS * 128]), ("ident_r", [128, 128]),
                       ("mask1_r", [4, HB]), ("mask2_r", [4, DB])]:
            hin(nm, sh)

        s = {}

        def sb(name, shape, dt):
            s[name] = es.enter_context(nc.sbuf_tensor(name, shape, dt))
        sb("x_res", [128, 3 * DB * Lc], F32)
        sb("t_str", [128, 3 * DB * Lc], F32)
        sb("xnf", [128, DB * Lc], F32)
        sb("xin_bf", [128, DB * Lc], BF16)
        sb("y_bf", [128, 2 * Lc], BF16)
        sb("xcin", [128, 2 * (Lc + 3)], F32)
        sb("tail3", [128, 2 * EB * 3], F32)
        sb("cconv", [128, 2 * Lc], F32)
        sb("sgtmp", [128, 2 * Lc], F32)
        sb("xc_bf", [128, 2 * EB * Lc], BF16)
        sb("zs_bf", [128, 2 * EB * Lc], BF16)
        sb("dtneg", [128, EB * Lc], BF16)
        sb("dtraw_bf", [64, Lc], BF16)
        sb("Bneg_bf", [16, Lc], BF16)
        sb("C_bf", [16, Lc], BF16)
        sb("B_bc", [128, NS * Lc], BF16)
        sb("C_bc", [128, NS * Lc], BF16)
        sb("u_bf", [128, 2 * Lc], BF16)
        sb("dA", [128, NS * Lc], F32)
        sb("dBu", [128, NS * Lc], BF16)
        sb("ytmp", [128, 2 * Lc], BF16)
        sb("carry", [128, 2 * EB * NS], F32)
        sb("hg_bf", [128, HB * Lc], BF16)
        sb("wbuf0", [128, DB * 128], BF16)
        sb("wbuf1", [128, DB * 128], BF16)
        sb("wbuf2", [128, 2048], BF16)
        sb("padt", [1, 16], F32)
        sb("xp_sb", [128, 2 * EB * 96], BF16)
        sb("dtp_s", [64, EB * 128], BF16)
        sb("Aneg", [128, 2 * EB * NS], F32)
        sb("ndtpb", [128, 2 * EB], F32)
        sb("cwsb", [128, 2 * EB * 4], F32)
        sb("cbc", [128, 2 * EB], F32)
        sb("Dsk", [128, 2 * EB], F32)
        sb("lnw", [128, 2 * DB], F32)
        sb("lnb", [128, 2 * DB], F32)
        sb("nfw", [128, DB], F32)
        sb("nfb", [128, DB], F32)
        sb("b1c", [128, HB], F32)
        sb("b2c", [128, DB], F32)
        sb("onesm", [128, 1], BF16)
        sb("epsc", [128, 1], F32)
        sb("nhalf", [128, 1], F32)
        sb("onesr_f", [1, 128], F32)
        sb("d16", [16, NS * 128], BF16)
        sb("sc4", [4, 1], F32)
        sb("sc16v", [16, 1], F32)
        sb("murow", [1, Lc], F32)
        sb("vrow", [1, Lc], F32)
        sb("rrow", [1, Lc], F32)
        sb("ident", [128, 128], BF16)
        sb("mask1", [4, HB], F32)
        sb("mask2", [4, DB], F32)
        sb("u_lt", [128, DB * 16], BF16)
        sb("T1sb", [4, D], BF16)
        sb("T2sb", [4, D], BF16)
        sb("coef_bf", [4, Lc], BF16)
        sb("zbt", [4, 8 * 128], BF16)
        sb("g4", [4, 4 * Lc], BF16)
        self.s = s

        raw = {}

        def psd(name, shape):
            raw[name] = es.enter_context(nc.psum_tensor(name, shape, F32))
        psd("ps_a", [128, Lc])
        psd("ps_b", [128, Lc])
        psd("ps_x", [16, 2 * Lc])
        psd("ps_y", [128, 2 * Lc])
        psd("ps_o", [128, DB * Lc])
        ps = {
            "ps_a": raw["ps_a"][:, :],
            "ps_b": raw["ps_b"][:, :],
            "ps_bm": raw["ps_x"][0:16, 0:Lc],
            "ps_cm": raw["ps_x"][0:16, Lc:2 * Lc],
            "ps_s1": raw["ps_x"][0:1, 0:Lc],
            "ps_s2": raw["ps_x"][0:1, Lc:2 * Lc],
            "ps_bc": raw["ps_y"][:, 0:Lc],
            "ps_bc2": raw["ps_y"][:, Lc:2 * Lc],
            "ps_dt": raw["ps_y"][0:64, Lc:2 * Lc],
            "ps_o": raw["ps_o"][:, :],
        }
        self.ps = ps
        for alias, owner in (("ps_bm", "ps_x"), ("ps_cm", "ps_x"),
                             ("ps_s1", "ps_x"), ("ps_s2", "ps_x"),
                             ("ps_bc", "ps_y"), ("ps_bc2", "ps_y"),
                             ("ps_dt", "ps_y")):
            self.dep[alias] = self.D(owner)

        self.p.alloc_sems(es)
        self.blk = es.enter_context(nc.Block())

    def bc8(self, psname):
        assert psname == "ps_bc"
        t = self.ps[psname].tensor
        return bass.AP(t, 0, [[2 * Lc, 128], [0, DB], [1, Lc]])

    def st0(self, ap_tensor, offset, reps, inner):
        return bass.AP(ap_tensor, offset, [[Lc, 128], [0, reps], [1, inner]])

    def body(self):
        self.p.scope = "prep"
        self.prepped = set()
        self.fifo = []
        self.wb2i = 0
        self.prescan_pushed = set()
        self.cur_pos = -1
        self.host_bdy = 0
        self.prep()
        nch = self.nch
        seq = [(0, 0)]
        for c in range(nch):
            if c + 1 < nch:
                seq.append((c + 1, 0))
            seq.append((c, 1))
        self.pos_of = {st: i for i, st in enumerate(seq)}
        self.xload(0)
        fa0, fb0 = self.ln0(0)
        fa0()
        fb0()
        self.ln0b(0)()
        self.push_prescan(0, 0, direct=True)
        if nch > 1:
            self.push_pack(1)
        self.bc_dt(0, 0)
        for i, (c, L_) in enumerate(seq):
            self.cur_pos = i
            self.host_bdy = 0
            self.p.scope = "bubble"
            self.drain_some(10 ** 9)
            nxt = seq[i + 1] if i + 1 < len(seq) else None
            if nxt is not None and nxt in self.prescan_pushed:
                self.push_bc_dt(nxt, host=(c, L_))
                nxt = None
            self.scanphase(c, L_)
            if L_ == 0:
                self.push_prescan(c, 1)
                if c == 1 and nch > 2:
                    self.push_pack(2)
            else:
                self.push_ffnblock(c)
                if c + 3 < nch:
                    self.push_pack(c + 3)
            if nxt is not None:
                self.push_bc_dt(nxt, host=(c, L_))
        self.cur_pos = len(seq)
        self.host_bdy = 0
        self.p.scope = "tail"
        self.drain_some(10 ** 9)
        self.p.scope = None
        p = self.p
        for st in p.DSTREAMS:
            v = p.dmacnt[st]
            if v:
                p.steps["sp"].append(
                    lambda h, sem=p.sems[st], v=v: h.wait_ge(sem, v))

    def push(self, fn, host=None, min_bdy=0):
        hp = self.pos_of[host] if host is not None else -1
        self.fifo.append([hp, min_bdy, fn])

    def drain_some(self, budget):
        n = 0
        while self.fifo and n < budget:
            hp, mb, fn = self.fifo[0]
            if hp > self.cur_pos or (hp == self.cur_pos and self.host_bdy < mb):
                break
            self.fifo.pop(0)
            fn()
            n += 1

    def push_pipe(self, entries, host=None):
        pend = None
        for d, cfn, mb in entries:
            if d is not None:
                self.push(d, host=host, min_bdy=max(0, mb - 1))
            if pend is not None:
                self.push(pend[0], host=host, min_bdy=pend[1])
            pend = (cfn, mb)
        if pend is not None:
            self.push(pend[0], host=host, min_bdy=pend[1])

    def tslot(self, c):
        sl = c % 3
        return (self.s["t_str"][:, sl * DB * Lc:(sl + 1) * DB * Lc],
                "t_str%d" % sl)

    def xslot(self, c):
        sl = c % 3
        return (self.s["x_res"][:, sl * DB * Lc:(sl + 1) * DB * Lc],
                "x_res%d" % sl)

    def prep(self):
        s, hbm, p = self.s, self.hbm, self.p
        op, dma = self.op, self.dma

        for src, dst in [("cw_r", "cwsb"), ("cb_r", "cbc"), ("Dsk_r", "Dsk"),
                         ("lnw_r", "lnw"), ("lnb_r", "lnb"), ("nfw_r", "nfw"),
                         ("nfb_r", "nfb"), ("b1_r", "b1c"), ("b2_r", "b2c"),
                         ("sc4_r", "sc4"), ("sc16_r", "sc16v"),
                         ("onesrow_r", "onesr_f"), ("mask1_r", "mask1"),
                         ("epsc_r", "epsc"), ("nhalf_r", "nhalf"),
                         ("mask2_r", "mask2")]:
            dma(s[dst][:, :], hbm[src], writes=(dst,))
        stg = s["dA"]
        dma(stg[:, 0:2 * EB * NS], hbm["Alog_r"], writes=("dA",))
        op("act", lambda h: h.activation(s["Aneg"][:, :], stg[:, 0:2 * EB * NS],
                                         AF.Exp),
           reads=("dA",), writes=("Aneg",))
        dma(stg[:, 0:2 * EB], hbm["dtpb_r"], writes=("dA",))
        op("dve", lambda h: h.tensor_scalar_mul(s["ndtpb"][:, :],
                                                stg[:, 0:2 * EB], -1.0),
           reads=("dA",), writes=("ndtpb",))
        dma(stg[:, 0:1], hbm["onesmean_r"], writes=("dA",))
        op("dve", lambda h: h.tensor_copy(s["onesm"][:, 0:1], stg[:, 0:1]),
           reads=("dA",), writes=("onesm",))
        dma(stg[0:16, 0:NS * 128], hbm["delta16_r"], writes=("dA",))
        op("dve", lambda h: h.tensor_copy(s["d16"][:, :], stg[0:16, 0:NS * 128]),
           reads=("dA",), writes=("d16",))
        dma(stg[:, 0:128], hbm["ident_r"], writes=("dA",))
        op("dve", lambda h: h.tensor_copy(s["ident"][:, :], stg[:, 0:128]),
           reads=("dA",), writes=("ident",))
        for L_, nm in ((0, "xpA_lhsT"), (1, "xpB_lhsT")):
            for k in range(EB):
                src2 = bass.AP(hbm[nm].tensor, k * 128 * 96, [[96, 128], [1, 96]])
                dma(stg[:, 0:96], src2, writes=("dA",))
                op("act", lambda h, L_=L_, k=k: h.activation(
                    s["xp_sb"][:, (L_ * EB + k) * 96:(L_ * EB + k + 1) * 96],
                    stg[:, 0:96], AF.Copy),
                   reads=("dA",), writes=("xp_sb",))
        for L_, nm in ((0, "dtpA_lhsT"), (1, "dtpB_lhsT")):
            dma(stg[0:64, 0:E], bass.AP(hbm[nm].tensor, 0, [[E, 64], [1, E]]),
                writes=("dA",))
            op("act", lambda h: h.activation(
                s["dBu"][:, :].bitcast(BF16)[0:64, 0:E],
                stg[0:64, 0:E], AF.Copy),
               reads=("dA",), writes=("dBu",))
            dma(bass.AP(self.scr_dtp.tensor, L_ * E, [[2 * E, 64], [1, E]]),
                s["dBu"][:, :].bitcast(BF16)[0:64, 0:E],
                reads=("dBu",), writes=("scr_dtp%d" % L_,))
        op("dve", lambda h: h.memset(s["tail3"][:, :], 0.0), writes=("tail3",))
        self.prep_slot = 0
        for nm in ("inA", "poA", "inB", "poB", "W1", "W2"):
            self.prep_big(nm)
        base = Dep()
        for nm2 in ("dA", "pstg0", "pstg1"):
            d0 = self.D(nm2)
            for e, v in d0.w.items():
                base.w[e] = max(base.w.get(e, 0), v)
            for e, v in d0.r.items():
                base.r[e] = max(base.r.get(e, 0), v)
        for i in range(2 * NS):
            d = Dep()
            d.w = dict(base.w)
            d.r = dict(base.r)
            self.dep["dAn%d" % i] = d
        tgt = self.D("dBu")
        for nm2 in ("pcv0", "pcv1"):
            d0 = self.D(nm2)
            for e, v in d0.w.items():
                tgt.w[e] = max(tgt.w.get(e, 0), v)
            for e, v in d0.r.items():
                tgt.r[e] = max(tgt.r.get(e, 0), v)

    def prep_big(self, nm):
        if nm in self.prepped:
            return
        self.prepped.add(nm)
        s, hbm, p = self.s, self.hbm, self.p
        op, dma = self.op, self.dma
        stg = s["dA"]
        oldscope = p.scope
        p.scope = "prep"
        owners_a = ["dA"] + ["dA%d" % n for n in range(NS)]
        owners_b = ["dBu"]
        for sl in range(4):
            for nm2, owners in (("pstg%d" % sl, owners_a), ("pcv%d" % sl, owners_b)):
                d = self.D(nm2)
                for ow in owners:
                    o = self.D(ow)
                    for e, v in o.w.items():
                        d.w[e] = max(d.w.get(e, 0), v)
                    for e, v in o.r.items():
                        d.r[e] = max(d.r.get(e, 0), v)
        if True:
            info = self.big[nm]
            lhsT, scr = hbm[nm + "_lhsT"], info["scr"]
            KB, Mm = info["KB"], info["M"]
            kmajor = nm in ("poA", "poB")
            units = []
            if kmajor:
                for k in range(KB):
                    units.append((bass.AP(lhsT.tensor, k * 128 * Mm,
                                          [[Mm, 128], [1, Mm]]),
                                  k * Mm, Mm))
            else:
                mg = 32 // KB
                gw = mg * 128
                ngrp = info["MB"] // mg
                per = min(2048 // gw, ngrp)
                for gg in range(ngrp // per):
                    for k in range(KB):
                        units.append((bass.AP(lhsT.tensor,
                                              k * 128 * Mm + gg * per * gw,
                                              [[Mm, 128], [1, per * gw]]),
                                      None, (gg, k, per, gw)))
            if kmajor:
                gsz = 2 * Mm
            else:
                gsz = KB * (32 // KB) * 128
            ldstreams = ("wb0", "wb1", "small")
            ststreams = ("prep", "xin", "out")
            for src2, dsto, meta in units:
                sl = self.prep_slot % 2
                six = self.prep_slot % 3
                self.prep_slot += 1
                if dsto is not None:
                    w = meta
                    ldsl = stg[:, sl * 2048:sl * 2048 + w]
                    cvsl = s["dBu"][:, sl * 2048:sl * 2048 + w]
                    dma(ldsl, src2, writes=("pstg%d" % sl,),
                        stream=ldstreams[six], queue="sp")
                    op("dve", lambda h, ldsl=ldsl, cvsl=cvsl: h.tensor_copy(
                        cvsl, ldsl),
                       reads=("pstg%d" % sl,), writes=("pcv%d" % sl,))
                    dst = bass.AP(scr.tensor, dsto, [[KB * Mm, 128], [1, w]])
                    gdep = "scr_%s_%d" % (nm, dsto // gsz)
                    dma(dst, cvsl, reads=("pcv%d" % sl,), writes=(gdep,),
                        stream=ststreams[six], queue="act")
                else:
                    gg, k, per, gw = meta
                    w = per * gw
                    ldsl = stg[:, sl * 2048:sl * 2048 + w]
                    cvsl = s["dBu"][:, sl * 2048:sl * 2048 + w]
                    dma(ldsl, src2, writes=("pstg%d" % sl,),
                        stream=ldstreams[six], queue="sp")
                    op("dve", lambda h, ldsl=ldsl, cvsl=cvsl: h.tensor_copy(
                        cvsl, ldsl),
                       reads=("pstg%d" % sl,), writes=("pcv%d" % sl,))
                    grp0 = gg * per
                    dst = bass.AP(scr.tensor,
                                  grp0 * (KB * gw) + k * gw,
                                  [[KB * Mm, 128], [KB * gw, per], [1, gw]])
                    gdeps = tuple("scr_%s_%d" % (nm, grp0 + j) for j in range(per))
                    dma(dst, cvsl, reads=("pcv%d" % sl,), writes=gdeps,
                        stream=ststreams[six], queue="act")
        for sl in range(4):
            for nm2 in ("pstg%d" % sl, "pcv%d" % sl):
                d = self.D(nm2)
                for tok_eng, tok_v in list(d.w.items()) + list(d.r.items()):
                    for ow in ["dA", "dBu"] + ["dA%d" % n for n in range(NS)]:
                        self.D(ow).did_rd((tok_eng, tok_v))

    def ln_stats(self, srcap, srcdep):
        self.ln_stats_a(srcap, srcdep)
        self.ln_stats_b()

    def ln_stats_a(self, srcap, srcdep):
        s, ps, op = self.s, self.ps, self.op
        op("act", lambda h: h.activation(s["xin_bf"][:, :], srcap, AF.Copy),
           reads=(srcdep,), writes=("xin_bf",))
        op("act", lambda h: h.activation(s["hg_bf"][:, 0:DB * Lc],
                                         s["xin_bf"][:, :], AF.Square),
           reads=("xin_bf",), writes=("hg_bf",))
        for k in range(DB):
            op("pe", lambda h, k=k: h.matmul(
                ps["ps_s1"][0:1, :], s["onesm"][:, 0:1],
                s["xin_bf"][:, k * Lc:(k + 1) * Lc],
                start=(k == 0), stop=(k == DB - 1)),
               reads=("onesm", "xin_bf"),
               writes=("ps_s1",) if k == DB - 1 else (),
               wchk=("ps_s1",) if k == 0 else ())
        for k in range(DB):
            op("pe", lambda h, k=k: h.matmul(
                ps["ps_s2"][0:1, :], s["onesm"][:, 0:1],
                s["hg_bf"][:, k * Lc:(k + 1) * Lc],
                start=(k == 0), stop=(k == DB - 1)),
               reads=("onesm", "hg_bf"),
               writes=("ps_s2",) if k == DB - 1 else (),
               wchk=("ps_s2",) if k == 0 else ())

    def ln_stats_b(self):
        s, ps, op = self.s, self.ps, self.op
        op("dve", lambda h: h.tensor_copy(s["murow"][0:1, :], ps["ps_s1"][0:1, :]),
           reads=("ps_s1",), writes=("murow",))
        op("dve", lambda h: h.tensor_mul(s["vrow"][0:1, :], s["murow"][0:1, :],
                                         s["murow"][0:1, :]),
           reads=("murow",), writes=("vrow",))
        op("dve", lambda h: h.tensor_sub(s["vrow"][0:1, :], ps["ps_s2"][0:1, :],
                                         s["vrow"][0:1, :]),
           reads=("ps_s2", "vrow"), writes=("vrow",))
        op("act", lambda h: h.activation(s["vrow"][0:1, :], s["vrow"][0:1, :],
                                         AF.Ln, bias=s["epsc"][0:1, 0:1]),
           reads=("vrow", "epsc"), writes=("vrow",))
        op("act", lambda h: h.activation(s["rrow"][0:1, :], s["vrow"][0:1, :],
                                         AF.Exp, scale=s["nhalf"][0:1, 0:1]),
           reads=("vrow", "nhalf"), writes=("rrow",))

    def ln_bcast(self, rowname, psname="ps_bc"):
        s, ps, op = self.s, self.ps, self.op
        op("pe", lambda h: h.matmul(ps[psname][:, :], s["onesr_f"][0:1, :],
                                    s[rowname][0:1, :], start=True, stop=True),
           reads=("onesr_f", rowname), writes=(psname,), wchk=(psname,))

    def xload(self, c):
        xap, xdep = self.xslot(c)
        src = bass.AP(self.hbm["x_t"].tensor, c * Lc,
                      [[Lseq, 128], [128 * Lseq, DB], [1, Lc]])
        self.dma(xap, src, writes=(xdep,), stream="xin")

    def ln0(self, c):
        def fa():
            self.p.scope = "ln0"
            xap, xdep = self.xslot(c)
            self.ln_stats_a(xap, xdep)

        def fb():
            self.p.scope = "ln0"
            self.ln_stats_b()
        return fa, fb

    def ln0b(self, c):
        def fn():
            self.p.scope = "ln0"
            s, op = self.s, self.op
            xap, xdep = self.xslot(c)
            tap_, tdep = self.tslot(c)
            self.ln_bcast("murow")
            op("dve", lambda h: h.tensor_sub(s["xnf"][:, :], xap,
                                             self.bc8("ps_bc")),
               reads=(xdep, "ps_bc"), writes=("xnf",))
            self.ln_bcast("rrow")
            op("dve", lambda h: h.tensor_mul(tap_, s["xnf"][:, :],
                                             self.bc8("ps_bc")),
               reads=("xnf", "ps_bc"), writes=(tdep,))
        return fn

    def push_pack(self, c):
        self.push(lambda c=c: self.xload(c))
        fa, fb = self.ln0(c)
        self.push(fa)
        self.push(fb)
        self.push(self.ln0b(c))
        self.push_prescan(c, 0)

    def lnin_u(self, c, L_):
        tap_, tdep = self.tslot(c)

        def f1a():
            self.p.scope = "lnin"
            self.ln_stats_a(tap_, tdep)

        def f1b():
            self.p.scope = "lnin"
            self.ln_stats_b()

        def f2():
            self.p.scope = "lnin"
            s, op = self.s, self.op
            lnoff = L_ * DB
            self.ln_bcast("murow")
            op("dve", lambda h: h.tensor_sub(s["xnf"][:, :], tap_,
                                             self.bc8("ps_bc")),
               reads=(tdep, "ps_bc"), writes=("xnf",))
            self.ln_bcast("rrow")
            op("dve", lambda h: h.tensor_mul(s["xnf"][:, :], s["xnf"][:, :],
                                             self.bc8("ps_bc")),
               reads=("xnf", "ps_bc"), writes=("xnf",))
            for k in range(DB):
                op("dve", lambda h, k=k: h.tensor_scalar(
                    s["xin_bf"][:, k * Lc:(k + 1) * Lc],
                    s["xnf"][:, k * Lc:(k + 1) * Lc],
                    s["lnw"][:, lnoff + k:lnoff + k + 1],
                    s["lnb"][:, lnoff + k:lnoff + k + 1],
                    ALU.mult, ALU.add),
                   reads=("xnf", "lnw", "lnb"), writes=("xin_bf",))
        return f1a, f1b, f2

    def push_prescan(self, c, L_, direct=False):
        host = (c - 1, L_) if (c - 1, L_) in self.pos_of else None
        f1a, f1b, f2 = self.lnin_u(c, L_)
        nm_in = "inA" if L_ == 0 else "inB"
        entries = [(None, f1a, 0), (None, f1b, 0), (None, f2, 0)]
        order = []
        for mblk in range(EB):
            for base in (0, EB):
                order.append((base + mblk, mblk + 1))
        prev = [None]

        def mk(ix):
            m, mb = order[ix]
            six = self.wb2i % 2
            self.wb2i += 1
            slot = six * 1024

            def dfn():
                self.p.scope = "inproj"
                s, dma = self.s, self.dma
                info = self.big[nm_in]
                KB = info["KB"]
                grp, j = divmod(m, 4)
                src = bass.AP(info["scr"].tensor, grp * (KB * 512) + j * 128,
                              [[KB * info["M"], 128], [512, KB], [1, 128]])
                dma(s["wbuf2"][:, slot:slot + KB * 128], src,
                    reads=("scr_%s_%d" % (nm_in, grp),),
                    writes=("wb2s%d" % six,), stream="wb2")

            def cfn():
                self.p.scope = "inproj"
                s, ps, op = self.s, self.ps, self.op
                info = self.big[nm_in]
                KB = info["KB"]
                bank = "ps_a" if m % 2 == 0 else "ps_b"
                if prev[0] is not None:
                    pm = prev[0]
                    pbank = "ps_a" if pm % 2 == 0 else "ps_b"
                    self.inproj_evac(c, L_, pm, pbank, L_ * EB)
                prev[0] = m
                for k in range(KB):
                    op("pe", lambda h, k=k, bank=bank: h.matmul(
                        ps[bank][:, :],
                        s["wbuf2"][:, slot + k * 128:slot + (k + 1) * 128],
                        s["xin_bf"][:, k * Lc:(k + 1) * Lc],
                        start=(k == 0), stop=(k == KB - 1)),
                       reads=("wb2s%d" % six, "xin_bf"),
                       writes=(bank,) if k == KB - 1 else (),
                       wchk=(bank,) if k == 0 else ())
            return dfn, cfn, mb

        for ix in range(len(order)):
            d, cf, mb = mk(ix)
            entries.append((d, cf, mb))

        def tail():
            self.p.scope = "inproj"
            pm = prev[0]
            pbank = "ps_a" if pm % 2 == 0 else "ps_b"
            self.inproj_evac(c, L_, pm, pbank, L_ * EB)
        entries.append((None, tail, EB))
        xa, xb = self.xproj_unit(c, L_)
        entries.append((None, xa, EB))
        entries.append((None, xb, EB))
        self.prescan_pushed.add((c, L_))
        if direct:
            for d, cfn, mb in entries:
                if d is not None:
                    d()
                cfn()
        else:
            self.push_pipe(entries, host=host)

    def xproj_unit(self, c, L_):
        def fn():
            self.p.scope = "xproj"
            s, ps, op = self.s, self.ps, self.op
            for k in range(EB):
                xoff = (L_ * EB + k) * 96
                for tgt, dep, c0, c1 in ((ps["ps_a"][0:64, :], "ps_a", 0, 64),
                                         (ps["ps_bm"][0:16, :], "ps_x", 64, 80),
                                         (ps["ps_b"][0:16, :], "ps_b", 80, 96)):
                    op("pe", lambda h, k=k, xoff=xoff, tgt=tgt, c0=c0, c1=c1:
                       h.matmul(tgt, s["xp_sb"][:, xoff + c0:xoff + c1],
                                s["xc_bf"][:, (L_ * EB + k) * Lc:
                                           (L_ * EB + k + 1) * Lc],
                                start=(k == 0), stop=(k == EB - 1)),
                       reads=("xp_sb", "xc%d" % L_),
                       writes=(dep,) if k == EB - 1 else (),
                       wchk=(dep,) if k == 0 else ())

        def fe():
            self.p.scope = "xproj"
            s, ps, op = self.s, self.ps, self.op
            op("act", lambda h: h.activation(s["dtraw_bf"][0:64, :],
                                             ps["ps_a"][0:64, :], AF.Copy),
               reads=("ps_a",), writes=("dtraw_bf",))
            op("dve", lambda h: h.tensor_scalar_mul(s["Bneg_bf"][0:16, :],
                                                    ps["ps_bm"][0:16, :], -1.0),
               reads=("ps_x",), writes=("Bneg_bf",))
            op("act", lambda h: h.activation(s["C_bf"][0:16, :],
                                             ps["ps_b"][0:16, :], AF.Copy),
               reads=("ps_b",), writes=("C_bf",))
        return fn, fe

    def xproj_unit_dead(self, c, L_):
        def fn():
            self.p.scope = "xproj"
            s, ps, op = self.s, self.ps, self.op
            for k in range(EB):
                xoff = (L_ * EB + k) * 96
                for tgt, dep, c0, c1 in ((ps["ps_a"][0:64, :], "ps_a", 0, 64),
                                         (ps["ps_bm"][0:16, :], "ps_x", 64, 80),
                                         (ps["ps_b"][0:16, :], "ps_b", 80, 96)):
                    op("pe", lambda h, k=k, xoff=xoff, tgt=tgt, c0=c0, c1=c1:
                       h.matmul(tgt, s["xp_sb"][:, xoff + c0:xoff + c1],
                                s["xc_bf"][:, (L_ * EB + k) * Lc:
                                           (L_ * EB + k + 1) * Lc],
                                start=(k == 0), stop=(k == EB - 1)),
                       reads=("xp_sb", "xc%d" % L_),
                       writes=(dep,) if k == EB - 1 else (),
                       wchk=(dep,) if k == 0 else ())
            op("act", lambda h: h.activation(s["dtraw_bf"][0:64, :],
                                             ps["ps_a"][0:64, :], AF.Copy),
               reads=("ps_a",), writes=("dtraw_bf",))
            op("dve", lambda h: h.tensor_scalar_mul(s["Bneg_bf"][0:16, :],
                                                    ps["ps_bm"][0:16, :], -1.0),
               reads=("ps_x",), writes=("Bneg_bf",))
            op("act", lambda h: h.activation(s["C_bf"][0:16, :],
                                             ps["ps_b"][0:16, :], AF.Copy),
               reads=("ps_b",), writes=("C_bf",))
        return fn

    def bc_dt(self, c, L_):
        for d, fn, mb in self.bc_dt_entries(c, L_):
            fn()

    def push_bc_dt(self, stage, host):
        c, L_ = stage
        for d, fn, mb in self.bc_dt_entries(c, L_):
            self.push(fn, host=host, min_bdy=mb)

    def bc_dt_entries(self, c, L_):
        s, ps, op = self.s, self.ps, self.op
        eoff = L_ * EB
        ents = []

        def dtp_load():
            self.p.scope = "dtproj"
            self.dma(s["dtp_s"][0:64, :],
                     bass.AP(self.scr_dtp.tensor, L_ * E, [[2 * E, 64], [1, E]]),
                     reads=("scr_dtp%d" % L_,), writes=("dtp_s",),
                     stream="small")
        ents.append((None, dtp_load, 14))

        def bcx():
            self.p.scope = "bcbuild"
            self._bcbuild(L_)
        ents.append((None, bcx, 16))

        def dtp():
            self.p.scope = "dtproj"
            for m in range(EB):
                bank = "ps_a" if m % 2 == 0 else "ps_b"
                op("pe", lambda h, m=m, bank=bank: h.matmul(
                    ps[bank][:, :], s["dtp_s"][0:64, m * 128:(m + 1) * 128],
                    s["dtraw_bf"][0:64, :], start=True, stop=True),
                   reads=("dtp_s", "dtraw_bf"), writes=(bank,), wchk=(bank,))
                op("act", lambda h, m=m, bank=bank: h.activation(
                    s["dtneg"][:, m * Lc:(m + 1) * Lc], ps[bank][:, :],
                    AF.Sigmoid,
                    scale=-1.0, bias=s["ndtpb"][:, eoff + m:eoff + m + 1]),
                   reads=(bank, "ndtpb"), writes=("dtneg",))
        ents.append((None, dtp, 16))

        def lnu():
            self.p.scope = "dtproj"
            op("act", lambda h: h.activation(s["dtneg"][:, :], s["dtneg"][:, :],
                                             AF.Ln),
               reads=("dtneg",), writes=("dtneg",))
            xco = L_ * EB * Lc
            op("dve", lambda h, xco=xco: h.tensor_mul(
                s["u_bf"][:, 0:Lc], s["dtneg"][:, 0:Lc],
                s["xc_bf"][:, xco:xco + Lc]),
               reads=("dtneg", "xc%d" % L_), writes=("u0",))
        ents.append((None, lnu, 16))
        return ents

    def _bcbuild(self, L_):
        s, ps, op = self.s, self.ps, self.op
        for n in range(NS):
            for psn, srcn, dstn in (("ps_bc", "Bneg_bf", "B_bc"),
                                    ("ps_bc2", "C_bf", "C_bc")):
                op("pe", lambda h, n=n, psn=psn, srcn=srcn: h.matmul(
                    ps[psn][:, :], s["d16"][:, n * 128:(n + 1) * 128],
                    s[srcn][0:16, :], start=True, stop=True),
                   reads=("d16", srcn), writes=(psn,), wchk=(psn,))
            for psn, srcn, dstn in (("ps_bc", "Bneg_bf", "B_bc"),
                                    ("ps_bc2", "C_bf", "C_bc")):
                op("act", lambda h, n=n, psn=psn, dstn=dstn: h.activation(
                    s[dstn][:, n * Lc:(n + 1) * Lc], ps[psn][:, :], AF.Copy),
                   reads=(psn,), writes=(dstn,))

    def stream_kgroups(self, nm, kg=2):
        info = self.big[nm]
        KB, MB = info["KB"], info["MB"]
        w = kg * MB * 128
        for grp in range(KB // kg):
            bufn = "wbuf%d" % (grp % 2)
            buf = self.s[bufn]
            src = bass.AP(info["scr"].tensor, grp * w,
                          [[KB * info["M"], 128], [1, w]])
            self.dma(buf[:, 0:w], src, reads=("scr_%s_%d" % (nm, grp),),
                     writes=(bufn,),
                     stream="wb0" if grp % 2 == 0 else "wb1")

            def tile(kig, m, buf=buf, MB=MB):
                off = kig * (MB * 128) + m * 128
                return buf[:, off:off + 128]
            yield grp, bufn, tile

    def scanphase(self, c, L_):
        s, ps, op = self.s, self.ps, self.op
        eoff = L_ * EB
        nm_po = "poA" if L_ == 0 else "poB"
        po_iter = self.stream_kgroups(nm_po, kg=1)
        for blk in range(EB):
            self.p.scope = "scan"
            self.scan_block(c, L_, blk, eoff)
            self.host_bdy = blk + 1
            self.drain_some(8)
            self.p.scope = "scan"
            grp, bufn, tile = next(po_iter)
            for m in range(DB):
                op("pe", lambda h, m=m, tile=tile, blk=blk: h.matmul(
                    ps["ps_o"][:, m * Lc:(m + 1) * Lc], tile(0, m),
                    s["y_bf"][:, (blk % 2) * Lc:(blk % 2 + 1) * Lc],
                    start=(blk == 0 and m % 2 == 0),
                    stop=(blk == EB - 1 and m % 2 == 1)),
                   reads=(bufn, "y%d" % (blk % 2)),
                   writes=("ps_o",) if (blk == EB - 1 and m == DB - 1) else (),
                   wchk=("ps_o",) if (blk == 0 and m == 0) else ())
        self.p.scope = "outproj"
        tap_, tdep = self.tslot(c)
        for m in range(DB):
            op("dve", lambda h, m=m: h.tensor_add(
                tap_[:, m * Lc:(m + 1) * Lc],
                tap_[:, m * Lc:(m + 1) * Lc],
                ps["ps_o"][:, m * Lc:(m + 1) * Lc]),
               reads=(tdep, "ps_o"), writes=(tdep,))

    def inproj_evac(self, c, L_, m, bank, eoff):
        s, ps, op = self.s, self.ps, self.op
        if m < EB:
            blk = m
            rot = blk % 2
            xoff = rot * (Lc + 3)
            op("dve", lambda h, blk=blk, xoff=xoff: h.tensor_copy(
                s["xcin"][:, xoff:xoff + 3],
                s["tail3"][:, (L_ * EB + blk) * 3:(L_ * EB + blk + 1) * 3]),
               reads=("tail3",), writes=("xcin%d" % rot,))
            op("dve", lambda h, xoff=xoff, bank=bank: h.tensor_copy(
                s["xcin"][:, xoff + 3:xoff + 3 + Lc], ps[bank][:, :]),
               reads=(bank,), writes=("xcin%d" % rot,))
            op("dve", lambda h, blk=blk, bank=bank: h.tensor_copy(
                s["tail3"][:, (L_ * EB + blk) * 3:(L_ * EB + blk + 1) * 3],
                ps[bank][:, Lc - 3:Lc]),
               reads=(bank,), writes=("tail3",))
            cvo = (L_ * EB + blk) * 4
            op("dve", lambda h, xoff=xoff, rot=rot, cvo=cvo, blk=blk:
               h.tensor_scalar(
                   s["cconv"][:, rot * Lc:(rot + 1) * Lc],
                   s["xcin"][:, xoff:xoff + Lc],
                   s["cwsb"][:, cvo:cvo + 1],
                   s["cbc"][:, eoff + blk:eoff + blk + 1],
                   ALU.mult, ALU.add),
               reads=("xcin%d" % rot, "cwsb", "cbc"), writes=("cconv%d" % rot,))
            for kk in range(1, 4):
                op("dve", lambda h, xoff=xoff, rot=rot, cvo=cvo, kk=kk:
                   h.scalar_tensor_tensor(
                       s["cconv"][:, rot * Lc:(rot + 1) * Lc],
                       s["xcin"][:, xoff + kk:xoff + kk + Lc],
                       s["cwsb"][:, cvo + kk:cvo + kk + 1],
                       s["cconv"][:, rot * Lc:(rot + 1) * Lc],
                       ALU.mult, ALU.add),
                   reads=("xcin%d" % rot, "cwsb", "cconv%d" % rot),
                   writes=("cconv%d" % rot,))
            op("act", lambda h, rot=rot: h.activation(
                s["sgtmp"][:, rot * Lc:(rot + 1) * Lc],
                s["cconv"][:, rot * Lc:(rot + 1) * Lc], AF.Sigmoid),
               reads=("cconv%d" % rot,), writes=("sgtmp%d" % rot,))
            op("dve", lambda h, rot=rot, blk=blk: h.tensor_mul(
                s["xc_bf"][:, (L_ * EB + blk) * Lc:(L_ * EB + blk + 1) * Lc],
                s["cconv"][:, rot * Lc:(rot + 1) * Lc],
                s["sgtmp"][:, rot * Lc:(rot + 1) * Lc]),
               reads=("cconv%d" % rot, "sgtmp%d" % rot),
               writes=("xc%d" % L_,))
        else:
            blk = m - EB
            rot = blk % 2
            op("act", lambda h, rot=rot, bank=bank: h.activation(
                s["sgtmp"][:, rot * Lc:(rot + 1) * Lc], ps[bank][:, :],
                AF.Sigmoid),
               reads=(bank,), writes=("sgtmp%d" % rot,))
            op("dve", lambda h, rot=rot, bank=bank, blk=blk: h.tensor_mul(
                s["zs_bf"][:, (L_ * EB + blk) * Lc:(L_ * EB + blk + 1) * Lc],
                ps[bank][:, :],
                s["sgtmp"][:, rot * Lc:(rot + 1) * Lc]),
               reads=(bank, "sgtmp%d" % rot), writes=("zs%d" % L_,))

    def scan_block(self, c, L_, blk, eoff):
        s, op = self.s, self.op
        rot = blk % 2
        aoff = (L_ * EB + blk) * NS
        dAfull = s["dA"][:, :].bitcast(BF16)
        dA = dAfull[:, rot * NS * Lc:(rot + 1) * NS * Lc]
        for n in range(NS):
            op("act", lambda h, n=n, blk=blk, aoff=aoff, dA=dA: h.activation(
                dA[:, n * Lc:(n + 1) * Lc],
                s["dtneg"][:, blk * Lc:(blk + 1) * Lc], AF.Exp,
                scale=s["Aneg"][:, aoff + n:aoff + n + 1]),
               reads=("dtneg", "Aneg"), writes=("dAn%d" % (rot * NS + n),))
        u0 = bass.AP(s["u_bf"][:, :].tensor, rot * Lc,
                     [[2 * Lc, 128], [0, NS], [1, Lc]])
        op("dve", lambda h, u0=u0: h.tensor_mul(s["dBu"][:, :], u0,
                                                s["B_bc"][:, :]),
           reads=("u%d" % rot, "B_bc"), writes=("dBu",))
        for n in range(NS):
            if c == 0:
                init = 0.0
            else:
                init = s["carry"][:, (aoff + n):(aoff + n) + 1]
            op("dve", lambda h, n=n, init=init, dA=dA: h.tensor_tensor_scan(
                s["dBu"][:, n * Lc:(n + 1) * Lc],
                dA[:, n * Lc:(n + 1) * Lc],
                s["dBu"][:, n * Lc:(n + 1) * Lc],
                init, ALU.mult, ALU.add),
               reads=("dAn%d" % (rot * NS + n), "dBu")
               + (("carry",) if c > 0 else ()),
               writes=("dBu",))
        if blk + 1 < EB:
            nrot = (blk + 1) % 2
            op("dve", lambda h, nrot=nrot, blk=blk: h.tensor_mul(
                s["u_bf"][:, nrot * Lc:(nrot + 1) * Lc],
                s["dtneg"][:, (blk + 1) * Lc:(blk + 2) * Lc],
                s["xc_bf"][:, (L_ * EB + blk + 1) * Lc:
                           (L_ * EB + blk + 2) * Lc]),
               reads=("dtneg", "xc%d" % L_), writes=("u%d" % nrot,))
        else:
            op("dve", lambda h, rot=rot: h.memset(
                s["ytmp"][:, (1 - rot) * Lc:(2 - rot) * Lc], 0.0),
               writes=("yt%d" % (1 - rot),))
        hlast = bass.AP(s["dBu"][:, :].tensor, Lc - 1, [[NS * Lc, 128], [Lc, NS]])
        op("dve", lambda h, hlast=hlast, aoff=aoff: h.tensor_copy(
            s["carry"][:, aoff:aoff + NS], hlast),
           reads=("dBu",), writes=("carry",))
        op("dve", lambda h: h.tensor_mul(s["dBu"][:, :], s["dBu"][:, :],
                                         s["C_bc"][:, :]),
           reads=("dBu", "C_bc"), writes=("dBu",))
        w_ = NS
        while w_ > 1:
            half = w_ // 2
            for j in range(half):
                op("dve", lambda h, j=j: h.tensor_add(
                    s["dBu"][:, j * Lc:(j + 1) * Lc],
                    s["dBu"][:, (2 * j) * Lc:(2 * j + 1) * Lc],
                    s["dBu"][:, (2 * j + 1) * Lc:(2 * j + 2) * Lc]),
                   reads=("dBu",), writes=("dBu",))
            w_ = half
        op("dve", lambda h, rot=rot, blk=blk: h.scalar_tensor_tensor(
            s["ytmp"][:, rot * Lc:(rot + 1) * Lc],
            s["xc_bf"][:, (L_ * EB + blk) * Lc:(L_ * EB + blk + 1) * Lc],
            s["Dsk"][:, eoff + blk:eoff + blk + 1],
            s["dBu"][:, 0:Lc], ALU.mult, ALU.add),
           reads=("xc%d" % L_, "Dsk", "dBu"), writes=("yt%d" % rot,))
        op("dve", lambda h, rot=rot, blk=blk: h.tensor_mul(
            s["y_bf"][:, rot * Lc:(rot + 1) * Lc],
            s["ytmp"][:, rot * Lc:(rot + 1) * Lc],
            s["zs_bf"][:, (L_ * EB + blk) * Lc:(L_ * EB + blk + 1) * Lc]),
           reads=("yt%d" % rot, "zs%d" % L_), writes=("y%d" % rot,))

    def lora_prep(self):
        s, ps, op = self.s, self.ps, self.op
        src = bass.AP(s["xin_bf"][:, :].tensor, 1,
                      [[DB * Lc, 128], [Lc, DB], [1, 16]])
        op("dve", lambda h, src=src: h.tensor_copy(s["u_lt"][:, :], src),
           reads=("xin_bf",), writes=("u_lt",))
        for tgt, base in (("T1sb", 0), ("T2sb", 8)):
            for blk in range(DB):
                inap = bass.AP(s["u_lt"][:, :].tensor, blk * 16 + base,
                               [[DB * 16, 128], [2, 4]])
                trout = ps["ps_bm"][0:4, 0:64].bitcast(BF16)
                op("pe", lambda h, inap=inap, trout=trout: h.transpose(
                    trout, inap, s["ident"][:, :]),
                   reads=("u_lt", "ident"), writes=("ps_bm",), wchk=("ps_bm",))
                op("act", lambda h, tgt=tgt, blk=blk, trout=trout: h.activation(
                    s[tgt][0:4, blk * 128:(blk + 1) * 128], trout, AF.Copy),
                   reads=("ps_bm",), writes=(tgt,))

    def push_ffnblock(self, c):
        s, ps, op = self.s, self.ps, self.op
        tap_, tdep = self.tslot(c)
        xap, xdep = self.xslot(c)
        host = (c + 2, 0) if (c + 2, 0) in self.pos_of else None

        def nf1a():
            self.p.scope = "normf"
            self.ln_stats_a(tap_, tdep)

        def nf1b():
            self.p.scope = "normf"
            self.ln_stats_b()

        def nf2():
            self.p.scope = "normf"
            self.ln_bcast("murow")
            op("dve", lambda h: h.tensor_sub(s["xnf"][:, :], tap_,
                                             self.bc8("ps_bc")),
               reads=(tdep, "ps_bc"), writes=("xnf",))
            self.ln_bcast("rrow")
            op("dve", lambda h: h.tensor_mul(s["xnf"][:, :], s["xnf"][:, :],
                                             self.bc8("ps_bc")),
               reads=("xnf", "ps_bc"), writes=("xnf",))
            for k in range(DB):
                op("dve", lambda h, k=k: h.tensor_scalar(
                    s["xnf"][:, k * Lc:(k + 1) * Lc],
                    s["xnf"][:, k * Lc:(k + 1) * Lc],
                    s["nfw"][:, k:k + 1], s["nfb"][:, k:k + 1],
                    ALU.mult, ALU.add),
                   reads=("xnf", "nfw", "nfb"), writes=("xnf",))
            op("dve", lambda h: h.tensor_add(xap, xap, s["xnf"][:, :]),
               reads=(xdep, "xnf"), writes=(xdep,))

        def lf1a():
            self.p.scope = "lnffn"
            self.ln_stats_a(xap, xdep)

        def lf1b():
            self.p.scope = "lnffn"
            self.ln_stats_b()

        def lf2():
            self.p.scope = "lnffn"
            self.ln_bcast("murow")
            op("dve", lambda h: h.tensor_sub(s["xnf"][:, :], xap,
                                             self.bc8("ps_bc")),
               reads=(xdep, "ps_bc"), writes=("xnf",))
            self.ln_bcast("rrow")
            op("dve", lambda h: h.tensor_mul(s["xin_bf"][:, :], s["xnf"][:, :],
                                             self.bc8("ps_bc")),
               reads=("xnf", "ps_bc"), writes=("xin_bf",))

        entries = [(None, nf1a), (None, nf1b), (None, nf2),
                   (None, lf1a), (None, lf1b), (None, lf2)]
        if c == 0:
            def lp():
                self.p.scope = "loraprep"
                self.lora_prep()
            entries.append((None, lp))

        def coefu():
            self.p.scope = "ffn"
            for k in range(DB):
                st = bass.AP(s["u_lt"][:, :].tensor, k * 16 + 1,
                             [[DB * 16, 128], [2, 4]])
                op("pe", lambda h, k=k, st=st: h.matmul(
                    ps["ps_bm"][0:4, :], st,
                    s["xin_bf"][:, k * Lc:(k + 1) * Lc],
                    start=(k == 0), stop=(k == DB - 1)),
                   reads=("u_lt", "xin_bf"),
                   writes=("ps_bm",) if k == DB - 1 else (),
                   wchk=("ps_bm",) if k == 0 else ())
            op("dve", lambda h: h.tensor_scalar_mul(s["coef_bf"][0:4, :],
                                                    ps["ps_bm"][0:4, :],
                                                    s["sc4"][0:4, 0:1]),
               reads=("ps_bm", "sc4"), writes=("coef_bf",))
        entries.append((None, coefu))

        w1info = self.big["W1"]
        prev1 = [None]

        def mk_w1(m):
            six = self.wb2i % 2
            self.wb2i += 1
            slot = six * 1024

            def dfn():
                self.p.scope = "ffn"
                grp, j = divmod(m, 4)
                KB = w1info["KB"]
                srcap = bass.AP(w1info["scr"].tensor,
                                grp * (KB * 512) + j * 128,
                                [[KB * w1info["M"], 128], [512, KB], [1, 128]])
                self.dma(s["wbuf2"][:, slot:slot + KB * 128], srcap,
                         reads=("scr_W1_%d" % grp,),
                         writes=("wb2s%d" % six,), stream="wb2")

            def gelu_of(pm):
                bank = "ps_a" if pm % 2 == 0 else "ps_b"
                op("act", lambda h, pm=pm, bank=bank: h.activation(
                    s["hg_bf"][:, pm * Lc:(pm + 1) * Lc], ps[bank][:, :],
                    AF.Gelu, bias=s["b1c"][:, pm:pm + 1]),
                   reads=(bank, "b1c"), writes=("hg_bf",))

            def cfn():
                self.p.scope = "ffn"
                KB = w1info["KB"]
                bank = "ps_a" if m % 2 == 0 else "ps_b"
                if prev1[0] is not None:
                    gelu_of(prev1[0])
                    prev1[0] = None
                for k in range(KB):
                    op("pe", lambda h, k=k, bank=bank: h.matmul(
                        ps[bank][:, :],
                        s["wbuf2"][:, slot + k * 128:slot + (k + 1) * 128],
                        s["xin_bf"][:, k * Lc:(k + 1) * Lc],
                        start=(k == 0), stop=False),
                       reads=("wb2s%d" % six, "xin_bf"),
                       wchk=(bank,) if k == 0 else ())
                zr = (m % 2) * 512
                op("dve", lambda h, m=m, zr=zr: h.tensor_scalar_mul(
                    s["zbt"][0:4, zr:zr + 128],
                    s["T1sb"][0:4, (m % 8) * 128:(m % 8 + 1) * 128],
                    s["mask1"][0:4, m:m + 1]),
                   reads=("T1sb", "mask1"), writes=("zbt%d" % (m % 2),))
                op("pe", lambda h, m=m, bank=bank, zr=zr: h.matmul(
                    ps[bank][:, :], s["zbt"][0:4, zr:zr + 128],
                    s["coef_bf"][0:4, :], start=False, stop=True),
                   reads=("zbt%d" % (m % 2), "coef_bf"), writes=(bank,))
                prev1[0] = m
            return dfn, cfn, gelu_of
        for m in range(2 * EB):
            d, cf, _ = mk_w1(m)
            entries.append((d, cf))

        def w1tail():
            self.p.scope = "ffn"
            pm = prev1[0]
            bank = "ps_a" if pm % 2 == 0 else "ps_b"
            op("act", lambda h, pm=pm, bank=bank: h.activation(
                s["hg_bf"][:, pm * Lc:(pm + 1) * Lc], ps[bank][:, :],
                AF.Gelu, bias=s["b1c"][:, pm:pm + 1]),
               reads=(bank, "b1c"), writes=("hg_bf",))
        entries.append((None, w1tail))
        for sp_ in range(4):
            entries.append((None, self.g_unit(sp_)))
        w2info = self.big["W2"]
        for m in range(DB):
            bank = "ps_a" if m % 2 == 0 else "ps_b"
            for q in range(4):
                six = self.wb2i % 2
                self.wb2i += 1
                slot = six * 1024

                def dfn(m=m, q=q, six=six, slot=slot):
                    self.p.scope = "ffn"
                    srcap = bass.AP(w2info["scr"].tensor, m * 4096 + q * 1024,
                                    [[32 * w2info["M"], 128], [1, 1024]])
                    self.dma(s["wbuf2"][:, slot:slot + 1024], srcap,
                             reads=("scr_W2_%d" % m,),
                             writes=("wb2s%d" % six,), stream="wb2")

                def cfn(m=m, q=q, six=six, slot=slot, bank=bank):
                    self.p.scope = "ffn"
                    for k in range(8):
                        kk = q * 8 + k
                        op("pe", lambda h, k=k, kk=kk, bank=bank, slot=slot:
                           h.matmul(
                               ps[bank][:, :],
                               s["wbuf2"][:, slot + k * 128:slot + (k + 1) * 128],
                               s["hg_bf"][:, kk * Lc:(kk + 1) * Lc],
                               start=(kk == 0), stop=False),
                           reads=("wb2s%d" % six, "hg_bf"),
                           wchk=(bank,) if kk == 0 else ())
                entries.append((dfn, cfn))

            def w2fin(m=m, bank=bank):
                self.p.scope = "ffn"
                zr = (m % 2) * 512
                for sp_ in range(4):
                    gat = bass.AP(s["T2sb"][:, :].tensor, (m % 2) * 512 + sp_,
                                  [[D, 4], [4, 128]])
                    op("dve", lambda h, m=m, sp_=sp_, gat=gat, zr=zr:
                       h.tensor_scalar_mul(
                           s["zbt"][0:4, zr + sp_ * 128:zr + (sp_ + 1) * 128],
                           gat, s["mask2"][0:4, m:m + 1]),
                       reads=("T2sb", "mask2"), writes=("zbt%d" % (m % 2),))
                for sp_ in range(4):
                    op("pe", lambda h, m=m, sp_=sp_, bank=bank, zr=zr: h.matmul(
                        ps[bank][:, :],
                        s["zbt"][0:4, zr + sp_ * 128:zr + (sp_ + 1) * 128],
                        s["g4"][0:4, sp_ * Lc:(sp_ + 1) * Lc],
                        start=False, stop=(sp_ == 3)),
                       reads=("zbt%d" % (m % 2), "g4"),
                       writes=(bank,) if sp_ == 3 else ())
                op("dve", lambda h, m=m, bank=bank: h.scalar_tensor_tensor(
                    xap[:, m * Lc:(m + 1) * Lc], ps[bank][:, :],
                    s["b2c"][:, m:m + 1],
                    xap[:, m * Lc:(m + 1) * Lc], ALU.add, ALU.add),
                   reads=(bank, "b2c", xdep), writes=(xdep,))
            entries.append((None, w2fin))

        def od():
            dst = bass.AP(self.out_h.tensor, c * Lc,
                          [[Lseq, 128], [128 * Lseq, DB], [1, Lc]])
            self.dma(dst, xap, reads=(xdep,), stream="out")
        entries.append((None, od))
        piped = [(d, f, 1 + i // 6) for i, (d, f) in enumerate(entries)]
        self.push_pipe(piped, host=host)

    def w1_unit(self, u):
        def fn():
            self.p.scope = "ffn"
            s, ps, op, dma = self.s, self.ps, self.op, self.dma
            info = self.big["W1"]
            KB = info["KB"]
            m0 = 2 * u
            grp = m0 // 4
            half = (m0 % 4) // 2
            src = bass.AP(info["scr"].tensor, grp * (KB * 512) + half * 256,
                          [[KB * info["M"], 128], [512, KB], [1, 256]])
            dma(s["wbuf2"][:, 0:KB * 256], src,
                reads=("scr_W1_%d" % grp,), writes=("wbuf2",), stream="wb2")
            for mi in range(2):
                m = m0 + mi
                bank = "ps_a" if m % 2 == 0 else "ps_b"
                for k in range(KB):
                    op("pe", lambda h, k=k, mi=mi, bank=bank: h.matmul(
                        ps[bank][:, :],
                        s["wbuf2"][:, k * 256 + mi * 128:k * 256 + mi * 128 + 128],
                        s["xin_bf"][:, k * Lc:(k + 1) * Lc],
                        start=(k == 0), stop=False),
                       reads=("wbuf2", "xin_bf"),
                       wchk=(bank,) if k == 0 else ())
                zr = (m % 2) * 512
                op("dve", lambda h, m=m, zr=zr: h.tensor_scalar_mul(
                    s["zbt"][0:4, zr:zr + 128],
                    s["T1sb"][0:4, (m % 8) * 128:(m % 8 + 1) * 128],
                    s["mask1"][0:4, m:m + 1]),
                   reads=("T1sb", "mask1"), writes=("zbt%d" % (m % 2),))
                op("pe", lambda h, m=m, bank=bank, zr=zr: h.matmul(
                    ps[bank][:, :], s["zbt"][0:4, zr:zr + 128],
                    s["coef_bf"][0:4, :], start=False, stop=True),
                   reads=("zbt%d" % (m % 2), "coef_bf"), writes=(bank,))
                op("act", lambda h, m=m, bank=bank: h.activation(
                    s["hg_bf"][:, m * Lc:(m + 1) * Lc], ps[bank][:, :], AF.Gelu,
                    bias=s["b1c"][:, m:m + 1]),
                   reads=(bank, "b1c"), writes=("hg_bf",))
        return fn

    def g_unit(self, sp_):
        def fn():
            self.p.scope = "ffn"
            s, op = self.s, self.op
            psn = ("ps_bm", "ps_cm", "ps_bc", "ps_dt")[sp_]
            for i, kb in enumerate(range(sp_ * 8, sp_ * 8 + 8)):
                st = bass.AP(s["u_lt"][:, :].tensor, (kb % 8) * 16 + 9,
                             [[DB * 16, 128], [2, 4]])
                op("pe", lambda h, kb=kb, st=st, psn=psn, i=i: h.matmul(
                    self.ps[psn][0:4, :],
                    st, s["hg_bf"][:, kb * Lc:(kb + 1) * Lc],
                    start=(i == 0), stop=(i == 7)),
                   reads=("u_lt", "hg_bf"), writes=(psn,) if i == 7 else (),
                   wchk=(psn,) if i == 0 else ())
            op("dve", lambda h, sp_=sp_, psn=psn: h.tensor_scalar_mul(
                s["g4"][0:4, sp_ * Lc:(sp_ + 1) * Lc],
                self.ps[psn][0:4, :],
                s["sc4"][0:4, 0:1]),
               reads=(psn, "sc4"), writes=("g4",))
        return fn

    def w2_unit(self, c, m):
        def fn():
            self.p.scope = "ffn"
            s, ps, op, dma = self.s, self.ps, self.op, self.dma
            info = self.big["W2"]
            xap, xdep = self.xslot(c)
            bank = "ps_a" if m % 2 == 0 else "ps_b"
            for hf in range(2):
                src = bass.AP(info["scr"].tensor, m * 4096 + hf * 2048,
                              [[32 * info["M"], 128], [1, 2048]])
                dma(s["wbuf2"][:, 0:2048], src,
                    reads=("scr_W2_%d" % m,), writes=("wbuf2",), stream="wb2")
                for k in range(16):
                    kk = hf * 16 + k
                    op("pe", lambda h, k=k, kk=kk, bank=bank: h.matmul(
                        ps[bank][:, :], s["wbuf2"][:, k * 128:(k + 1) * 128],
                        s["hg_bf"][:, kk * Lc:(kk + 1) * Lc],
                        start=(kk == 0), stop=False),
                       reads=("wbuf2", "hg_bf"),
                       wchk=(bank,) if kk == 0 else ())
            zr = (m % 2) * 512
            for sp_ in range(4):
                gat = bass.AP(s["T2sb"][:, :].tensor, (m % 2) * 512 + sp_,
                              [[D, 4], [4, 128]])
                op("dve", lambda h, m=m, sp_=sp_, gat=gat, zr=zr:
                   h.tensor_scalar_mul(
                       s["zbt"][0:4, zr + sp_ * 128:zr + (sp_ + 1) * 128],
                       gat, s["mask2"][0:4, m:m + 1]),
                   reads=("T2sb", "mask2"), writes=("zbt%d" % (m % 2),))
            for sp_ in range(4):
                op("pe", lambda h, m=m, sp_=sp_, bank=bank, zr=zr: h.matmul(
                    ps[bank][:, :],
                    s["zbt"][0:4, zr + sp_ * 128:zr + (sp_ + 1) * 128],
                    s["g4"][0:4, sp_ * Lc:(sp_ + 1) * Lc],
                    start=False, stop=(sp_ == 3)),
                   reads=("zbt%d" % (m % 2), "g4"),
                   writes=(bank,) if sp_ == 3 else ())
            op("dve", lambda h, m=m, bank=bank: h.scalar_tensor_tensor(
                xap[:, m * Lc:(m + 1) * Lc], ps[bank][:, :],
                s["b2c"][:, m:m + 1],
                xap[:, m * Lc:(m + 1) * Lc], ALU.add, ALU.add),
               reads=(bank, "b2c", xdep), writes=(xdep,))
        return fn


def make_inputs(inp):
    f = np.float32
    A = np.ascontiguousarray
    outs = []
    const = {
        "sc4_r": np.repeat(inp["L0_ffn_scale"].reshape(1, 1), 4, 0).astype(f),
        "sc16_r": np.repeat(inp["L0_ffn_scale"].reshape(1, 1), 16, 0).astype(f),
        "onesmean_r": np.full((128, 1), 1.0 / D, f),
        "epsc_r": np.full((128, 1), EPS, f),
        "nhalf_r": np.full((128, 1), -0.5, f),
        "onesrow_r": np.ones((1, 128), f),
        "ident_r": np.eye(128, dtype=f),
    }
    d16 = np.zeros((16, NS * 128), f)
    for n in range(NS):
        d16[n, n * 128:(n + 1) * 128] = 1.0
    const["delta16_r"] = d16
    m1 = np.zeros((4, HB), f)
    for b2 in range(HB):
        m1[b2 // 8, b2] = 1.0
    const["mask1_r"] = m1
    m2 = np.zeros((4, DB), f)
    for b in range(DB):
        m2[b // 2, b] = 1.0
    const["mask2_r"] = m2

    def r3(x, nb):
        return A(x.reshape(2, nb, 128).transpose(2, 0, 1).reshape(128, 2 * nb))

    def r1(x, nb):
        return A(x.reshape(nb, 128).T)

    w = {
        "inA_lhsT": A(inp["L0_attn_in_proj_w"][0].T),
        "inB_lhsT": A(inp["L0_attn_in_proj_w"][1].T),
        "poA_lhsT": A(inp["L0_attn_out_proj_w"][0].T),
        "poB_lhsT": A(inp["L0_attn_out_proj_w"][1].T),
        "W1_lhsT": A(inp["L0_ffn_W1"].T),
        "W2_lhsT": A(inp["L0_ffn_W2"].T),
        "xpA_lhsT": A(inp["L0_attn_x_proj_w"][0].T),
        "xpB_lhsT": A(inp["L0_attn_x_proj_w"][1].T),
        "dtpA_lhsT": A(inp["L0_attn_dt_proj_w"][0].T),
        "dtpB_lhsT": A(inp["L0_attn_dt_proj_w"][1].T),
        "Alog_r": A(inp["L0_attn_A_log"].reshape(2, EB, 128, NS)
                    .transpose(2, 0, 1, 3).reshape(128, 2 * EB * NS)),
        "dtpb_r": r3(inp["L0_attn_dt_proj_b"], EB),
        "cw_r": A(inp["L0_attn_conv_w"].reshape(2, EB, 128, 4)
                  .transpose(2, 0, 1, 3).reshape(128, 2 * EB * 4)),
        "cb_r": r3(inp["L0_attn_conv_b"], EB),
        "Dsk_r": r3(inp["L0_attn_D_skip"], EB),
        "lnw_r": r3(inp["L0_attn_ln_w"], DB),
        "lnb_r": r3(inp["L0_attn_ln_b"], DB),
        "nfw_r": r1(inp["L0_attn_norm_f_w"], DB),
        "nfb_r": r1(inp["L0_attn_norm_f_b"], DB),
        "b1_r": r1(inp["L0_ffn_b1"], HB),
        "b2_r": r1(inp["L0_ffn_b2"], DB),
    }
    w.update(const)
    x = inp["L0_attn_x"]
    in_maps = []
    for b in range(x.shape[0]):
        m = dict(w)
        m["x_t"] = A(x[b].T)
        in_maps.append(m)
    return in_maps


def kernel(**inputs):
    in_maps = make_inputs(inputs)
    nc, _ = build()
    res = run_bass_kernel_spmd(nc, in_maps, core_ids=list(range(len(in_maps))))
    return np.stack([np.ascontiguousarray(r["out_t"].T) for r in res.results])
